# version 02
# baseline (speedup 1.0000x reference)
import os
import numpy as np
import ml_dtypes
import concourse.bass as bass
import concourse.mybir as mybir
from concourse.bass_utils import run_bass_kernel_spmd

F32 = mybir.dt.float32
BF16 = mybir.dt.bfloat16
I32 = mybir.dt.int32
AF = mybir.ActivationFunctionType
ALU = mybir.AluOpType
AX = mybir.AxisListType

NCORES = 8
BEL = 2
T = 2048
E = 1024
NH = 16
DH = 4096
TT = 256
NTILE = T // TT
NC_E = E // 128
NC_H = DH // 128
NKT = T // 128
QSCALE = 1.0 / 32.0

ENGS = ("sync", "tensor", "vector", "scalar", "gpsimd")


class Bld:
    def __init__(self, nc):
        self.nc = nc
        names = ["sd", "sq", "sg", "st", "sv", "ss", "sp"]
        self.sem = {n: nc.alloc_semaphore(name=n) for n in names}
        self.cnt = {n: 0 for n in names}
        self.ops = {e: [] for e in ENGS}
        self.water = {e: {n: 0 for n in names} for e in ENGS}
        self.own = {"tensor": "st", "vector": "sv", "scalar": "ss", "gpsimd": "sp"}
        self.drain_mark = {e: 0 for e in ENGS}

    def tok(self, sem):
        return (sem, self.cnt[sem])

    def barrier(self):
        return [(n, self.cnt[n]) for n in self.cnt]

    def op(self, eng, fn, waits=(), inc=None, inc_n=1):
        own = self.own.get(eng)
        if inc is None and own is not None and fn is not None:
            inc = own
        w = []
        need_drain = False
        for t in waits:
            if t is None:
                continue
            s, v = t
            if v <= 0:
                continue
            if s == own:
                if v > self.drain_mark[eng]:
                    need_drain = True
                continue
            if v > self.water[eng][s]:
                w.append((s, v))
                self.water[eng][s] = v
        if need_drain:
            self.ops[eng].append(([], lambda e: e.drain(), None, 0))
            self.drain_mark[eng] = self.cnt[own]
        tok = None
        if inc is not None:
            self.cnt[inc] += inc_n
            tok = (inc, self.cnt[inc])
        self.ops[eng].append((w, fn, inc, inc_n))
        return tok

    def dma(self, queue, out, in_, waits=(), transpose=False, slow=False):
        sem = {"sync": "sd", "scalar": "sq", "gpsimd": "sg"}[queue]
        if transpose:
            fn = lambda e, o=out, i=in_: e.dma_start_transpose(out=o, in_=i)
        elif slow:
            fn = lambda e, o=out, i=in_: e.dma_start(
                out=o, in_=i, allow_slow_non_contiguous=True)
        else:
            fn = lambda e, o=out, i=in_: e.dma_start(out=o, in_=i)
        return self.op(queue, fn, waits=waits, inc=sem, inc_n=16)

    def finalize(self):
        sem = self.sem
        with self.nc.Block() as block:
            def mk(name):
                def body(eng):
                    for (w, fn, inc, n) in self.ops[name]:
                        for (s, v) in w:
                            eng.wait_ge(sem[s], v)
                        if fn is None:
                            continue
                        inst = fn(eng)
                        if inc is not None and inst is not None:
                            inst.then_inc(sem[inc], n)
                return body
            block.sync(mk("sync"))
            block.tensor(mk("tensor"))
            block.vector(mk("vector"))
            block.scalar(mk("scalar"))
            block.gpsimd(mk("gpsimd"))


def build_nc(dbg=(), phases="full"):
    WLVL = int(os.environ.get("WLVL", "12"))
    BLVL = int(os.environ.get("BLVL", "9"))
    BTT = int(os.environ.get("BTT", "8"))
    BCO = int(os.environ.get("BCO", "8"))
    BSB = int(os.environ.get("BSB", "2"))
    nc = bass.Bass("TRN2", target_bir_lowering=False)
    B = Bld(nc)

    xin = nc.dram_tensor("x", [BEL, T, E], F32, kind="ExternalInput").ap()
    wq_ = nc.dram_tensor("wq", [E, E], F32, kind="ExternalInput").ap()
    wk_ = nc.dram_tensor("wk", [E, E], F32, kind="ExternalInput").ap()
    wo_ = nc.dram_tensor("wo", [E, E], F32, kind="ExternalInput").ap()
    bo_ = nc.dram_tensor("bo", [E], F32, kind="ExternalInput").ap()
    w1_ = nc.dram_tensor("w1", [DH, E], F32, kind="ExternalInput").ap()
    cw_ = nc.dram_tensor("cw", [DH, 3], F32, kind="ExternalInput").ap()
    cb_ = nc.dram_tensor("cb", [DH], F32, kind="ExternalInput").ap()
    w2_ = nc.dram_tensor("w2", [E, DH], F32, kind="ExternalInput").ap()
    idb_ = nc.dram_tensor("idb", [128, 128], BF16, kind="ExternalInput").ap()
    idf_ = nc.dram_tensor("idf", [128, 128], F32, kind="ExternalInput").ap()
    ones_ = nc.dram_tensor("ones", [128, 128], F32, kind="ExternalInput").ap()
    yout = nc.dram_tensor("y", [BEL, T, E], F32, kind="ExternalOutput").ap()
    scrw1 = nc.dram_tensor("scrw1", [E, DH], BF16, kind="Internal").ap()
    scrw2 = nc.dram_tensor("scrw2", [DH, E], BF16, kind="Internal").ap()
    scrx = nc.dram_tensor("scrx", [BEL, T, E], F32, kind="Internal").ap()
    dbg_t = {}
    for (name, shape, dt_) in dbg:
        dbg_t[name] = nc.dram_tensor(name, shape, dt_, kind="ExternalOutput").ap()

    def sb(name, shape, dtype, **kw):
        return nc.alloc_sbuf_tensor(name, shape, dtype, **kw).ap()

    idb = sb("idb_s", [128, 128], BF16)
    idf = sb("idf_s", [128, 128], F32)
    onesf = sb("ones_s", [128, 128], F32)
    bo_sb = sb("bo_s", [128, 8], F32)
    cw_sb = sb("cw_s", [128, 96], F32)
    cb_sb = sb("cb_s", [128, 32], F32)
    wA = sb("wA", [128, 8192], BF16)
    wB = sb("wB", [128, 8192], BF16)
    arnQ = sb("arnQ", [128, 16384], BF16, is_transpose_dest=True)
    arnK = sb("arnK", [128, 16384], BF16)
    arnV = sb("arnV", [128, 16768], BF16)
    arnX = sb("arnX", [128, 16384], BF16, is_transpose_dest=True)
    arnG = sb("arnG", [128, 4096], F32)
    arnH = sb("arnH", [128, 3072], F32)
    arnS = sb("arnS", [128, 1024], F32)
    arnP = sb("arnP", [128, 2048], BF16)
    trst = sb("trst", [128, 256], BF16)
    wsb = wA[:, 0:4096]
    wsb2 = wA[:, 4096:8192]

    psA = nc.alloc_psum_tensor("psA", [128, 2048], F32).ap()
    psB = nc.alloc_psum_tensor("psB", [128, 2048], F32).ap()

    g_f = arnK.bitcast(F32)
    h_f = arnV.bitcast(F32)
    v_tm = arnV
    x_tmbf = arnV

    def rsqrt_chain(vcol, scol, tmpc, waits):
        t0 = B.op("scalar", lambda e: e.activation(out=tmpc[0], in_=vcol,
                  func=AF.Sqrt), waits=waits, inc="ss")
        a1 = B.op("vector", lambda e: e.reciprocal(out=tmpc[1], in_=tmpc[0]),
                  waits=[t0], inc="sv")
        a2 = B.op("vector", lambda e: e.tensor_tensor(out=tmpc[2], in0=vcol,
                  in1=tmpc[1], op=ALU.mult), waits=[a1], inc="sv")
        a3 = B.op("vector", lambda e: e.tensor_tensor(out=tmpc[1], in0=tmpc[0],
                  in1=tmpc[2], op=ALU.add), waits=[a2], inc="sv")
        a4 = B.op("vector", lambda e: e.tensor_scalar(out=tmpc[0], in0=tmpc[1],
                  scalar1=0.5, scalar2=None, op0=ALU.mult), waits=[a3], inc="sv")
        tk = B.op("vector", lambda e: e.reciprocal(out=scol, in_=tmpc[0]),
                  waits=[a4], inc="sv")
        return tk

    B.dma("sync", idb[:], idb_[:])
    B.dma("sync", idf[:], idf_[:])
    B.dma("sync", onesf[:], ones_[:])
    B.dma("sync", bo_sb[:, 0:8], bo_.rearrange("(c p) -> p c", p=128), slow=True)
    B.dma("sync", cw_sb[:, 0:96].rearrange("p (hc t) -> p hc t", t=3),
          cw_.rearrange("(hc p) t -> p hc t", p=128))
    const_tok = B.dma("sync", cb_sb[:, 0:32], cb_.rearrange("(hc p) -> p hc", p=128),
                      slow=True)
    t_wq = B.dma("gpsimd", wA[:].rearrange("p (a b) -> p a b", a=8), wq_.rearrange("(ci p) o -> p ci o", p=128))
    t_wk = B.dma("gpsimd", wB[:].rearrange("p (a b) -> p a b", a=8), wk_.rearrange("(ci p) o -> p ci o", p=128))

    wstg = arnQ.bitcast(F32)[:, 0:8192]
    wqf = arnX.bitcast(F32)[:, 0:8192]
    wqi = arnK.bitcast(I32)[:, 0:8192]
    wqb_t = arnV[:, 0:16384]

    def wquant(widx, src_of_rb, dst_tile):
        s0 = 4 * widx
        sums = arnS[:, s0:s0 + 4]
        if WLVL < 2:
            return
        for rb in range(4):
            tl = B.dma("sync", wstg[:].rearrange("p (a b) -> p a b", a=8), src_of_rb(rb),
                       waits=[B.tok("sv")] if rb > 0 else [])
            B.op("vector", lambda e, rb=rb: e.tensor_reduce(
                out=sums[:, rb:rb + 1], in_=wstg[:], axis=AX.X, op=ALU.add,
                apply_absolute_value=True), waits=[tl], inc="sv")
        tv = B.tok("sv")
        if WLVL < 3:
            return
        t1 = B.op("tensor", lambda e: e.matmul(
            out=psB[0:1, 0:4], lhsT=onesf[:, 0:1], rhs=sums[:, 0:4],
            start=True, stop=True), waits=[tv, const_tok], inc="st")
        c0 = 16 + 16 * widx
        tot, mean, meanc = arnS[0:1, c0:c0 + 1], arnS[0:1, c0 + 1:c0 + 2], arnS[0:1, c0 + 2:c0 + 3]
        scale, scale_r = arnS[0:1, c0 + 3:c0 + 4], arnS[0:1, c0 + 4:c0 + 5]
        b1 = B.op("vector", lambda e: e.tensor_reduce(out=tot, in_=psB[0:1, 0:4],
                  axis=AX.X, op=ALU.add), waits=[t1], inc="sv")
        b2 = B.op("vector", lambda e: e.tensor_scalar(out=mean, in0=tot,
                  scalar1=1.0 / (DH * E), scalar2=None, op0=ALU.mult),
                  waits=[b1], inc="sv")
        b3 = B.op("vector", lambda e: e.tensor_scalar(out=meanc, in0=mean,
                  scalar1=1e-5, scalar2=None, op0=ALU.max), waits=[b2], inc="sv")
        b4 = B.op("vector", lambda e: e.reciprocal(out=scale, in_=meanc),
                  waits=[b3], inc="sv")
        tvr = B.op("vector", lambda e: e.reciprocal(out=scale_r, in_=scale),
                   waits=[b4], inc="sv")
        t2 = B.op("tensor", lambda e: e.matmul(
            out=psB[:, 8:9], lhsT=onesf[0:1, 0:128], rhs=scale,
            start=True, stop=True), waits=[tvr], inc="st")
        t3 = B.op("tensor", lambda e: e.matmul(
            out=psB[:, 520:521], lhsT=onesf[0:1, 0:128], rhs=scale_r,
            start=True, stop=True), waits=[tvr], inc="st")
        scol, srcol = arnS[:, c0 + 5:c0 + 6], arnS[:, c0 + 6:c0 + 7]
        B.op("vector", lambda e: e.tensor_copy(out=scol, in_=psB[:, 8:9]),
             waits=[t2], inc="sv")
        tbc = B.op("vector", lambda e: e.tensor_copy(out=srcol, in_=psB[:, 520:521]),
                   waits=[t3], inc="sv")
        if WLVL < 4:
            return
        tev = {}
        dtok = {}
        last_tt = None
        for rb in range(4):
            tl = B.dma("sync", wstg[:].rearrange("p (a b) -> p a b", a=8), src_of_rb(rb),
                       waits=[tbc, B.tok("sv"), B.tok("sp")])
            tm1 = B.op("vector", lambda e: e.tensor_scalar(
                out=wqf[:], in0=wstg[:], scalar1=scol, scalar2=None,
                op0=ALU.mult), waits=[tl, tbc], inc="sv")
            tm2 = B.op("vector", lambda e: e.tensor_copy(out=wqi[:], in_=wqf[:]),
                       waits=[tm1], inc="sv")
            if WLVL < 5:
                continue
            tm3 = B.op("gpsimd", lambda e: e.tensor_scalar(
                out=wqi[:], in0=wqi[:], scalar1=1, scalar2=None,
                op0=ALU.min), waits=[tm2], inc="sp")
            tm4 = B.op("gpsimd", lambda e: e.tensor_scalar(
                out=wqi[:], in0=wqi[:], scalar1=-1, scalar2=None,
                op0=ALU.max), waits=[tm3], inc="sp")
            tq = B.op("gpsimd", lambda e: e.tensor_copy(
                out=wqb_t[:, 0:8192], in_=wqi[:]),
                waits=[tm4] + ([last_tt] if last_tt else []), inc="sp")
            if WLVL < 6:
                continue
            for k in range(64):
                j, dc = k // 8, k % 8
                slot = k % 2
                pv = psB[:, 1024 + slot * 512:1024 + slot * 512 + 64].bitcast(BF16)
                w = [tq]
                if (rb, k - 2) in tev:
                    w.append(tev[(rb, k - 2)])
                elif rb > 0 and k < 2:
                    w.append(tev[(rb - 1, 62 + k)])
                last_tt = B.op("tensor", lambda e, j=j, dc=dc, pv=pv: e.transpose(
                    out=pv, in_=wqb_t[:, j * 1024 + dc * 128:j * 1024 + dc * 128 + 128],
                    identity=idb[:]), waits=w, inc="st")
                if WLVL < 7:
                    tev[(rb, k)] = last_tt
                    dtok[(rb, k)] = last_tt
                    continue
                we = [last_tt]
                if (rb, k - 2) in dtok:
                    we.append(dtok[(rb, k - 2)])
                elif rb > 0 and k < 2:
                    we.append(dtok[(rb - 1, 62 + k)])
                tev[(rb, k)] = B.op("vector", lambda e, slot=slot, pv=pv: e.tensor_copy(
                    out=trst[:, slot * 128:slot * 128 + 128], in_=pv),
                    waits=we, inc="sv")
                if WLVL < 8:
                    dtok[(rb, k)] = tev[(rb, k)]
                    continue
                dtok[(rb, k)] = B.dma("sync", dst_tile(rb, j, dc),
                                      trst[:, slot * 128:slot * 128 + 128],
                                      waits=[tev[(rb, k)]])

    wquant(0,
           lambda rb: w1_[rb * 1024:(rb + 1) * 1024, :].rearrange("(j p) d -> p j d", p=128),
           lambda rb, j, dc: scrw1[dc * 128:dc * 128 + 128,
                                   rb * 1024 + j * 128:rb * 1024 + j * 128 + 128])
    if WLVL >= 9:
        wquant(1,
           lambda rb: w2_[:, rb * 1024:(rb + 1) * 1024].rearrange("(j p) h -> p j h", p=128),
               lambda rb, j, dc: scrw2[rb * 1024 + dc * 128:rb * 1024 + dc * 128 + 128,
                                       j * 128:j * 128 + 128])

    w1sr_col = arnS[:, 22:23]
    w2sr_col = arnS[:, 38:39]
    B.op("vector", lambda e: e.tensor_scalar(
        out=cw_sb[:, 0:96], in0=cw_sb[:, 0:96], scalar1=w1sr_col, scalar2=None,
        op0=ALU.mult), waits=[const_tok, B.tok("sv")], inc="sv")

    out_toks = {}
    psb_rd = {2: None, 3: None}
    nel = 0 if phases == "w" else (1 if phases != "full" else BEL)
    for el in range(nel):
        bar = B.barrier()

        tcast = []
        for tc in range(NKT):
            tcast.append(B.dma("gpsimd", x_tmbf[:, tc * 1024:(tc + 1) * 1024],
                               xin[el, tc * 128:(tc + 1) * 128, :], waits=bar))
        for tc in range(NKT):
            for ci in range(NC_E):
                xbf_tok = B.dma("sync",
                                arnX[:, ci * 2048 + tc * 128:ci * 2048 + tc * 128 + 128],
                                x_tmbf[:, tc * 1024 + ci * 128:tc * 1024 + ci * 128 + 128],
                                waits=bar + [tcast[tc]], transpose=True)

        if phases == "a0":
            continue
        if el > 0:
            t_wq = B.dma("gpsimd", wA[:].rearrange("p (a b) -> p a b", a=8),
                         wq_.rearrange("(ci p) o -> p ci o", p=128), waits=bar)
        evac_glob = {}
        q_last_mm = None
        for wi, (wten, dst, scl, wtok) in enumerate(
                [(wA, arnQ, QSCALE, t_wq), (wB, arnK, 1.0, t_wk)]):
            for co in range(NC_E):
                gidx = wi * NC_E + co
                ps = psA if (gidx % 2 == 0) else psB
                mmw = [xbf_tok, wtok, evac_glob.get(gidx - 2)] + bar
                lt = None
                for ci in range(NC_E):
                    for seg in range(4):
                        lt = B.op("tensor", lambda e, wten=wten, ci=ci, co=co, seg=seg, ps=ps: e.matmul(
                            out=ps[:, seg * 512:(seg + 1) * 512],
                            lhsT=wten[:, ci * 1024 + co * 128:ci * 1024 + co * 128 + 128],
                            rhs=arnX[:, ci * 2048 + seg * 512:ci * 2048 + (seg + 1) * 512],
                            start=(ci == 0), stop=(ci == NC_E - 1),
                            skip_group_check=True), waits=mmw, inc="st")
                evac_glob[gidx] = B.op("scalar", lambda e, dst=dst, co=co, scl=scl, ps=ps: e.activation(
                    out=dst[:, co * 2048:(co + 1) * 2048], in_=ps[:, 0:2048],
                    func=AF.Identity, scale=scl), waits=[lt], inc="ss")
            if wi == 0:
                q_last_mm = lt
        qk_tok = B.tok("ss")
        t_wo = B.dma("gpsimd", wA[:].rearrange("p (a b) -> p a b", a=8),
                     wo_.rearrange("(ci p) o -> p ci o", p=128),
                     waits=[q_last_mm])

        B.op("vector", lambda e: e.tensor_reduce(out=arnS[:, 40:41], in_=arnQ[:],
             axis=AX.X, op=ALU.max, apply_absolute_value=True),
             waits=[qk_tok], inc="sv")
        tmk = B.op("vector", lambda e: e.tensor_reduce(out=arnS[:, 41:42], in_=arnK[:],
                   axis=AX.X, op=ALU.max, apply_absolute_value=True),
                   waits=[qk_tok], inc="sv")
        t4 = B.op("tensor", lambda e: e.transpose(out=psB[0:2, 1024:1152],
                  in_=arnS[:, 40:42], identity=idf[:]), waits=[tmk], inc="st")
        t5 = B.op("vector", lambda e: e.tensor_reduce(out=arnS[0:2, 43:44],
                  in_=psB[0:2, 1024:1152], axis=AX.X, op=ALU.max),
                  waits=[t4], inc="sv")
        t6 = B.op("tensor", lambda e: e.transpose(out=psB[0:1, 1152:1154],
                  in_=arnS[0:2, 43:44], identity=idf[0:2, 0:2]), waits=[t5], inc="st")
        c1_ = B.op("vector", lambda e: e.tensor_copy(out=arnS[0:1, 47:49],
                   in_=psB[0:1, 1152:1154]), waits=[t6], inc="sv")
        c2_ = B.op("vector", lambda e: e.tensor_tensor(out=arnS[0:1, 44:45],
                   in0=arnS[0:1, 47:48], in1=arnS[0:1, 48:49], op=ALU.mult),
                   waits=[c1_], inc="sv")
        t7 = B.op("vector", lambda e: e.tensor_scalar(out=arnS[0:1, 45:46],
                  in0=arnS[0:1, 44:45], scalar1=-64.0, scalar2=None,
                  op0=ALU.mult), waits=[c2_], inc="sv")
        t8 = B.op("tensor", lambda e: e.matmul(out=psB[:, 1154:1155],
                  lhsT=onesf[0:1, 0:128], rhs=arnS[0:1, 45:46],
                  start=True, stop=True), waits=[t7], inc="st")
        negB_col = arnS[:, 46:47]
        negB_tok = B.op("vector", lambda e: e.tensor_copy(out=negB_col,
                        in_=psB[:, 1154:1155]), waits=[t8], inc="sv")

        if phases == "a1":
            continue
        v4 = v_tm[:, 0:16640].rearrange("p (kt h d) -> p kt h d", kt=NKT, h=NH)
        tms0 = B.op("vector", lambda e: e.memset(v_tm[:, 16640:16768], 0.0),
                    waits=[B.tok("sd"), qk_tok], inc="sv")
        tms = B.op("vector", lambda e: e.memset(v4[:, :, :, 64:65], 1.0),
                   waits=[B.tok("sd"), qk_tok, tms0], inc="sv")
        vt_ev = {}
        k2 = 0
        for ci in range(NC_E):
            for kt in range(NKT):
                slot = k2 % 2
                pv = psB[:, 1024 + slot * 512:1024 + slot * 512 + 64].bitcast(BF16)
                w = [qk_tok, tms, negB_tok, vt_ev.get(k2 - 2)]
                tt_ = B.op("tensor", lambda e, ci=ci, kt=kt, pv=pv: e.transpose(
                    out=pv, in_=arnK[:, ci * 2048 + kt * 128:ci * 2048 + kt * 128 + 128],
                    identity=idb[:]), waits=w, inc="st")
                B.op("vector", lambda e, ci=ci, kt=kt, pv=pv: e.tensor_copy(
                    out=v_tm[:, kt * 1040 + (2 * ci) * 65:kt * 1040 + (2 * ci) * 65 + 64],
                    in_=pv[:, 0:64]), waits=[tt_], inc="sv")
                vt_ev[k2] = B.op("vector", lambda e, ci=ci, kt=kt, pv=pv: e.tensor_copy(
                    out=v_tm[:, kt * 1040 + (2 * ci + 1) * 65:kt * 1040 + (2 * ci + 1) * 65 + 64],
                    in_=pv[:, 64:128]), waits=[tt_], inc="sv")
                k2 += 1
        vtm_tok = vt_ev[k2 - 1]

        if phases == "a2":
            continue
        exp_hist = []
        pv_hist = []
        tail_sv = None
        tail_sg = None
        attn_tails = []
        for h in range(NH):
            ci, off = h // 2, 64 * (h % 2)
            tp = (off, 0)
            pv_last = None
            for qh in range(2):
                for kt in range(NKT):
                    ps = psA[:, (kt % 2) * 1024:(kt % 2) * 1024 + 1024]
                    ptb = arnP[:, (kt % 2) * 1024:(kt % 2) * 1024 + 1024]
                    sw = [qk_tok, negB_tok, vtm_tok]
                    if len(exp_hist) >= 2:
                        sw.append(exp_hist[-2])
                    lt = None
                    for seg in range(2):
                        lt = B.op("tensor", lambda e, ci=ci, off=off, kt=kt, qh=qh, seg=seg, ps=ps, tp=tp: e.matmul(
                            out=ps[:, seg * 512:(seg + 1) * 512],
                            lhsT=arnK[off:off + 64, ci * 2048 + kt * 128:ci * 2048 + kt * 128 + 128],
                            rhs=arnQ[off:off + 64, ci * 2048 + qh * 1024 + seg * 512:ci * 2048 + qh * 1024 + (seg + 1) * 512],
                            start=True, stop=True, tile_position=tp,
                            skip_group_check=True), waits=sw, inc="st")
                    ew = [lt, negB_tok]
                    if len(pv_hist) >= 2:
                        ew.append(pv_hist[-2])
                    et = B.op("scalar", lambda e, ptb=ptb, ps=ps: e.activation(
                        out=ptb, in_=ps, func=AF.Exp, bias=negB_col, scale=1.0),
                        waits=ew, inc="ss")
                    exp_hist.append(et)
                    pw = [et]
                    if kt == 0 and qh == 0:
                        pw += [vtm_tok, tail_sv, tail_sg]
                    for seg in range(2):
                        pv_last = B.op("tensor", lambda e, h=h, kt=kt, qh=qh, seg=seg, ptb=ptb: e.matmul(
                            out=psB[0:128, qh * 1024 + seg * 512:qh * 1024 + (seg + 1) * 512],
                            lhsT=v_tm[:, kt * 1040 + h * 65:kt * 1040 + h * 65 + 128],
                            rhs=ptb[:, seg * 512:(seg + 1) * 512],
                            start=(kt == 0), stop=(kt == NKT - 1),
                            skip_group_check=True), waits=pw, inc="st")
                    pv_hist.append(pv_last)
            tmp = arnG[0:64, 0:2048]
            r_row = arnG[64:65, 0:2048]
            r_bc = arnG[0:64, 2048:4096]
            te = B.op("scalar", lambda e: e.activation(
                out=tmp, in_=psB[0:64, 0:2048], func=AF.Identity),
                waits=[pv_last, tail_sv, tail_sg], inc="ss")
            tr = B.op("vector", lambda e: e.reciprocal(
                out=r_row, in_=psB[64:65, 0:2048]), waits=[pv_last], inc="sv")
            tbc2 = None
            for rnd in range(2):
                bt = None
                for seg in range(2):
                    gseg = rnd * 2 + seg
                    bt = B.op("tensor", lambda e, seg=seg, gseg=gseg: e.matmul(
                        out=psB[0:64, 1024 + seg * 512:1024 + (seg + 1) * 512],
                        lhsT=onesf[64:65, 0:64],
                        rhs=r_row[0:1, gseg * 512:(gseg + 1) * 512],
                        start=True, stop=True, skip_group_check=True,
                        tile_position=(64, 0)),
                        waits=[te, tr, tbc2], inc="st")
                tbc2 = B.op("vector", lambda e, rnd=rnd: e.tensor_copy(
                    out=r_bc[:, rnd * 1024:(rnd + 1) * 1024],
                    in_=psB[0:64, 1024:2048]),
                    waits=[bt, tail_sg, psb_rd[2], psb_rd[3]], inc="sv")
                psb_rd[2] = tbc2
                psb_rd[3] = tbc2
            tail_sv = B.op("vector", lambda e: e.tensor_tensor(
                out=r_bc, in0=tmp, in1=r_bc, op=ALU.mult), waits=[tbc2], inc="sv")
            tail_sg = B.dma("gpsimd",
                            arnX[off:off + 64, ci * 2048:(ci + 1) * 2048],
                            r_bc, waits=[tail_sv])
            attn_tails.append(tail_sg)
        attn_tok = attn_tails[-1]
        attn_sv = tail_sv

        if el == 0 and "q" in dbg_t:
            B.dma("scalar", dbg_t["q"][0:128, :], arnQ[:, :], waits=[qk_tok])
            B.dma("scalar", dbg_t["k"][0:128, :], arnK[:, :], waits=[qk_tok])
        if el == 0 and "attn" in dbg_t:
            B.dma("scalar", dbg_t["attn"][0:128, :], arnX[:, :], waits=[attn_tok])

        if phases == "a3":
            continue
        st_ = arnS
        xq_first = True
        spills = {}
        wo_evac = {}
        for tt in range(NTILE):
            def wo_ps(co):
                c0 = (co % 4) * 512 + (co // 4) * 256
                return psA[:, c0:c0 + 256]
            for co in range(NC_E):
                lt = None
                for ci in range(NC_E):
                    w = [attn_tok, t_wo, wo_evac.get(co), wo_evac.get(co - 4), attn_sv]
                    lt = B.op("tensor", lambda e, co=co, ci=ci, tt=tt: e.matmul(
                        out=wo_ps(co),
                        lhsT=wA[:, ci * 1024 + co * 128:ci * 1024 + co * 128 + 128],
                        rhs=arnX[:, ci * 2048 + tt * 256:ci * 2048 + (tt + 1) * 256],
                        start=(ci == 0), stop=(ci == NC_E - 1)),
                        waits=w, inc="st")
                wo_evac[co] = B.op("scalar", lambda e, co=co: e.activation(
                    out=arnG[:, co * 256:co * 256 + 256],
                    in_=wo_ps(co),
                    func=AF.Identity, bias=bo_sb[:, co:co + 1]),
                    waits=[lt], inc="ss")
            if BLVL < 2 or tt >= BTT:
                continue
            for sb2 in range(BSB):
                tb = tt * 2 + sb2
                asm = arnH[:, (tb % 2) * 1024:(tb % 2) * 1024 + 1024]
                xstage = arnH[:, 2048:3072]
                cps = []
                for co in range(min(NC_E, BCO)):
                    slot = co % 2
                    pv = psB[:, 1024 + slot * 512:1024 + slot * 512 + 128]
                    w = [wo_evac[co], psb_rd[2 + slot]]
                    ttr = B.op("tensor", lambda e, co=co, sb2=sb2, pv=pv: e.transpose(
                        out=pv, in_=arnG[:, co * 256 + sb2 * 128:co * 256 + sb2 * 128 + 128],
                        identity=idf[:]), waits=w, inc="st")
                    cw_ = [ttr, spills.get(tb - 2)]
                    if tt == 0 and sb2 == 0 and co == 0:
                        cw_ += [out_toks.get(0), out_toks.get(1)]
                    cp = B.op("vector", lambda e, co=co, pv=pv, asm=asm: e.tensor_copy(
                        out=asm[:, co * 128:co * 128 + 128], in_=pv),
                        waits=cw_, inc="sv")
                    cps.append(cp)
                    psb_rd[2 + slot] = cp
                if BLVL < 3:
                    continue
                txl = B.dma("sync", xstage, xin[el, tb * 128:(tb + 1) * 128, :],
                            waits=[B.tok("sv")])
                if BCO < NC_E:
                    continue
                ta = B.op("vector", lambda e, asm=asm, xstage=xstage: e.tensor_tensor(
                    out=asm, in0=asm, in1=xstage, op=ALU.add),
                    waits=[txl] + cps, inc="sv")
                if BLVL < 4:
                    continue
                x2 = arnG[:, 2048:3072]
                ta2 = B.op("vector", lambda e, asm=asm, x2=x2: e.tensor_tensor(
                    out=x2, in0=asm, in1=asm, op=ALU.mult), waits=[ta, attn_tok], inc="sv")
                d1 = B.op("vector", lambda e, asm=asm: e.tensor_reduce(
                    out=st_[:, 50:51], in_=asm, axis=AX.X, op=ALU.add),
                    waits=[ta], inc="sv")
                d2 = B.op("vector", lambda e, x2=x2: e.tensor_reduce(
                    out=st_[:, 51:52], in_=x2, axis=AX.X, op=ALU.add),
                    waits=[ta2], inc="sv")
                d3 = B.op("vector", lambda e: e.tensor_scalar(out=st_[:, 52:53],
                     in0=st_[:, 50:51], scalar1=1.0 / E, scalar2=None,
                     op0=ALU.mult), waits=[d1], inc="sv")
                d4 = B.op("vector", lambda e: e.tensor_scalar(out=st_[:, 53:54],
                     in0=st_[:, 51:52], scalar1=1.0 / E, scalar2=None,
                     op0=ALU.mult), waits=[d2], inc="sv")
                d5 = B.op("vector", lambda e: e.tensor_tensor(out=st_[:, 54:55],
                     in0=st_[:, 52:53], in1=st_[:, 52:53], op=ALU.mult),
                     waits=[d3], inc="sv")
                d6 = B.op("vector", lambda e: e.tensor_tensor(out=st_[:, 55:56],
                     in0=st_[:, 53:54], in1=st_[:, 54:55], op=ALU.subtract),
                     waits=[d4, d5], inc="sv")
                tve = B.op("vector", lambda e: e.tensor_scalar(out=st_[:, 56:57],
                           in0=st_[:, 55:56], scalar1=1e-5, scalar2=None,
                           op0=ALU.add), waits=[d6], inc="sv")
                rstd_tok = rsqrt_chain(st_[:, 56:57], st_[:, 61:62],
                                       [st_[:, 57:58], st_[:, 58:59], st_[:, 59:60]],
                                       [tve])
                tln = B.op("vector", lambda e, asm=asm: e.tensor_scalar(
                    out=asm, in0=asm, scalar1=st_[:, 52:53], scalar2=st_[:, 61:62],
                    op0=ALU.subtract, op1=ALU.mult), waits=[rstd_tok], inc="sv")
                if BLVL < 5:
                    continue
                spills[tb] = B.dma("sync", scrx[el, tb * 128:(tb + 1) * 128, :],
                                   asm, waits=[tln])
                e1 = B.op("vector", lambda e, asm=asm: e.tensor_reduce(
                    out=st_[:, 62:63], in_=asm, axis=AX.X, op=ALU.max,
                    apply_absolute_value=True), waits=[tln], inc="sv")
                e2 = B.op("vector", lambda e: e.tensor_scalar(out=st_[:, 63:64],
                     in0=st_[:, 62:63], scalar1=1e-5, scalar2=None,
                     op0=ALU.max), waits=[e1], inc="sv")
                e3 = B.op("vector", lambda e: e.reciprocal(out=st_[:, 64:65],
                     in_=st_[:, 63:64]), waits=[e2], inc="sv")
                tsc = B.op("vector", lambda e: e.tensor_scalar(out=st_[:, 65:66],
                           in0=st_[:, 64:65], scalar1=127.0, scalar2=None,
                           op0=ALU.mult), waits=[e3], inc="sv")
                tinv = B.op("vector", lambda e: e.tensor_scalar(out=st_[:, 66:67],
                            in0=st_[:, 63:64], scalar1=1.0 / 127.0, scalar2=None,
                            op0=ALU.mult), waits=[e2], inc="sv")
                qf = arnG[:, 3072:4096]
                qi = arnG[:, 2048:3072].bitcast(I32)
                tq1 = B.op("vector", lambda e, asm=asm, qf=qf: e.tensor_scalar(
                    out=qf, in0=asm, scalar1=st_[:, 65:66], scalar2=None,
                    op0=ALU.mult), waits=[tsc], inc="sv")
                tq2 = B.op("vector", lambda e, qf=qf, qi=qi: e.tensor_copy(
                    out=qi, in_=qf), waits=[tq1], inc="sv")
                xqs = arnP[:, (tb % 2) * 1024:(tb % 2) * 1024 + 1024]
                tq3 = B.op("vector", lambda e, qi=qi, xqs=xqs: e.tensor_scalar(
                    out=xqs, in0=qi, scalar1=st_[:, 66:67], scalar2=None,
                    op0=ALU.mult), waits=[tq2, tinv], inc="sv")
                if BLVL < 6:
                    continue
                for ci in range(NC_E):
                    w = [tq3]
                    if xq_first:
                        w += [B.tok("st"), B.tok("sg")]
                        xq_first = False
                    xq_tok = B.dma("sync",
                                   arnQ[:, ci * 2048 + tb * 128:ci * 2048 + tb * 128 + 128],
                                   xqs[:, ci * 128:ci * 128 + 128],
                                   waits=w, transpose=True)

        if el == 0 and "w2qt" in dbg_t:
            B.dma("scalar", dbg_t["w2qt"][0:128, :], scrw2[0:128, :],
                  waits=[xq_tok])
        if el == 0 and "w1qt" in dbg_t:
            B.dma("scalar", dbg_t["w1qt"][0:128, :], scrw1[0:128, 0:1024],
                  waits=[xq_tok])
        if el == 0 and "xq" in dbg_t:
            B.dma("scalar", dbg_t["xq"][0:128, :], arnQ[:, :], waits=[xq_tok])
        if el == 0 and "xres" in dbg_t:
            B.dma("scalar", dbg_t["xres"][:, :], scrx[0, :, :],
                  waits=[spills[tb2] for tb2 in range(NKT)])
        if el == 0 and "stats" in dbg_t:
            B.dma("scalar", dbg_t["stats"][0:128, :], arnS[:, :], waits=[B.tok("sv")])
        if phases == "b":
            continue
        wo_done = B.tok("st")
        dbg_gate = []
        w1slot_user = {}
        w2slot_user = {}
        gqb_user = {}
        y1_evac = {}
        y2_evac_prev = []

        def ybank(oc):
            return (psA if oc < 4 else psB)[:, (oc % 4) * 512:(oc % 4) * 512 + 256]

        for tau in range(NTILE):
            jlo = 1 if tau == 0 else 0
            jhi = 257 if tau == NTILE - 1 else 258
            tok0 = tau * 256 - 1
            hv = h_f[:, 0:NC_H * 258].rearrange("p (hc j) -> p hc j", hc=NC_H)
            if tau == 0:
                tmz = B.op("vector", lambda e, hv=hv: e.memset(hv[:, :, 0:1], 0.0),
                           waits=[xq_tok], inc="sv")
            elif tau == NTILE - 1:
                tmz = B.op("vector", lambda e, hv=hv: e.memset(hv[:, :, 257:258], 0.0),
                           inc="sv")
            else:
                tmz = None
            mp_tok = None
            conv_done = {}
            for hc in range(NC_H):
                slot = hc % 4
                tw1 = B.dma("sync",
                            wsb[:, slot * 1024:slot * 1024 + 1024].rearrange("p (a b) -> p a b", a=8),
                            scrw1[:, hc * 128:hc * 128 + 128].rearrange("(ci p) h -> p ci h", p=128),
                            waits=[w1slot_user.get((hc - 2) % 4),
                                   (wo_done if tau == 0 and hc < 4 else None),
                                   bar[0] if el == 0 else None])
                psY1 = psA[:, (hc % 2) * 512:(hc % 2) * 512 + 512]
                mmw = [tw1, xq_tok, y1_evac.get(hc % 2)]
                if tau == 1 and hc == 0:
                    mmw += dbg_gate
                if tau > 0 and hc == 0:
                    mmw += y2_evac_prev
                if tau == 0 and hc == 0:
                    mmw.append(B.tok("ss"))
                lt = None
                for ci in range(NC_E):
                    lt = B.op("tensor", lambda e, ci=ci, psY1=psY1, jlo=jlo, jhi=jhi, tok0=tok0, slot=slot: e.matmul(
                        out=psY1[:, jlo:jhi],
                        lhsT=wsb[:, slot * 1024 + ci * 128:slot * 1024 + ci * 128 + 128],
                        rhs=arnQ[:, ci * 2048 + tok0 + jlo:ci * 2048 + tok0 + jhi],
                        start=(ci == 0), stop=(ci == NC_E - 1)),
                        waits=mmw, inc="st")
                w1slot_user[slot] = lt
                hew = [lt, tmz]
                hev = B.op("scalar", lambda e, hc=hc, psY1=psY1, jlo=jlo, jhi=jhi: e.activation(
                    out=h_f[:, hc * 258 + jlo:hc * 258 + jhi], in_=psY1[:, jlo:jhi],
                    func=AF.Identity), waits=hew, inc="ss")
                y1_evac[hc % 2] = hev
                cva = arnG[:, 256:514]
                cvb = arnG[:, 514:772]
                c1 = B.op("vector", lambda e, hc=hc, cva=cva: e.tensor_scalar(
                    out=cva[:, 0:256], in0=h_f[:, hc * 258:hc * 258 + 256],
                    scalar1=cw_sb[:, hc * 3:hc * 3 + 1], scalar2=cb_sb[:, hc:hc + 1],
                    op0=ALU.mult, op1=ALU.add), waits=[hev], inc="sv")
                c2 = B.op("vector", lambda e, hc=hc, cva=cva, cvb=cvb: e.scalar_tensor_tensor(
                    out=cvb[:, 0:256], in0=h_f[:, hc * 258 + 1:hc * 258 + 257],
                    scalar=cw_sb[:, hc * 3 + 1:hc * 3 + 2], in1=cva[:, 0:256],
                    op0=ALU.mult, op1=ALU.add), waits=[c1], inc="sv")
                c3 = B.op("vector", lambda e, hc=hc, cva=cva, cvb=cvb: e.scalar_tensor_tensor(
                    out=cva[:, 0:256], in0=h_f[:, hc * 258 + 2:hc * 258 + 258],
                    scalar=cw_sb[:, hc * 3 + 2:hc * 3 + 3], in1=cvb[:, 0:256],
                    op0=ALU.mult, op1=ALU.add), waits=[c2], inc="sv")
                gsl = B.op("scalar", lambda e, hc=hc, cva=cva: e.activation(
                    out=g_f[:, hc * 256:hc * 256 + 256], in_=cva[:, 0:256],
                    func=AF.Silu), waits=[c3], inc="ss")
                conv_done[hc] = gsl
                mpart = arnG[:, 0:256]
                gabs = arnG[:, 1284:1540]
                tab = B.op("scalar", lambda e, hc=hc, gabs=gabs: e.activation(
                    out=gabs, in_=g_f[:, hc * 256:hc * 256 + 256], func=AF.Abs),
                    waits=[gsl, mp_tok], inc="ss")
                if hc == 0:
                    mp_tok = B.op("vector", lambda e, mpart=mpart, gabs=gabs: e.tensor_copy(
                        out=mpart, in_=gabs), waits=[tab], inc="sv")
                else:
                    mp_tok = B.op("vector", lambda e, mpart=mpart, gabs=gabs: e.tensor_tensor(
                        out=mpart, in0=mpart, in1=gabs, op=ALU.max),
                        waits=[tab, mp_tok], inc="sv")
            if el == 0 and tau == 0 and "h" in dbg_t:
                dbg_gate.append(B.dma("scalar", dbg_t["h"][0:128, :],
                                      h_f[:, 0:8256], waits=[y1_evac[0], y1_evac[1]]))
            if el == 0 and tau == 0 and "g" in dbg_t:
                dbg_gate.append(B.dma("scalar", dbg_t["g"][0:128, :],
                                      g_f[:, 0:8192], waits=[conv_done[NC_H - 1], mp_tok]))
            rowtoks = []
            for half in range(2):
                pv = psB[:, 1024 + half * 512:1024 + half * 512 + 128]
                w = [mp_tok, psb_rd[2 + half]] + y2_evac_prev
                t9 = B.op("tensor", lambda e, half=half, pv=pv: e.transpose(
                    out=pv, in_=arnG[:, half * 128:half * 128 + 128],
                    identity=idf[:]), waits=w, inc="st")
                rt = B.op("vector", lambda e, half=half, pv=pv: e.tensor_reduce(
                    out=st_[:, 70 + half:71 + half], in_=pv, axis=AX.X, op=ALU.max),
                    waits=[t9], inc="sv")
                rowtoks.append(rt)
                psb_rd[2 + half] = rt
            f1 = B.op("vector", lambda e: e.tensor_scalar(out=st_[:, 72:74],
                 in0=st_[:, 70:72], scalar1=1e-5, scalar2=None, op0=ALU.max),
                 waits=rowtoks, inc="sv")
            f2 = B.op("vector", lambda e: e.reciprocal(out=st_[:, 74:76],
                 in_=st_[:, 72:74]), waits=[f1], inc="sv")
            tsc2 = B.op("vector", lambda e: e.tensor_scalar(out=st_[:, 76:78],
                        in0=st_[:, 74:76], scalar1=127.0, scalar2=None,
                        op0=ALU.mult), waits=[f2], inc="sv")
            trs2 = B.op("vector", lambda e: e.tensor_scalar(out=st_[:, 78:80],
                        in0=st_[:, 72:74], scalar1=w2sr_col, scalar2=1.0 / 127.0,
                        op0=ALU.mult, op1=ALU.mult), waits=[f1], inc="sv")
            srow = arnS[0:1, 128:384]
            rrow = arnS[0:1, 384:640]
            bctoks = []
            for (cbase, tokn, row) in ((76, tsc2, srow), (78, trs2, rrow)):
                for half in range(2):
                    off11 = 1280 + (half % 2) * 512
                    t11 = B.op("tensor", lambda e, cbase=cbase, half=half, off11=off11: e.transpose(
                        out=psB[0:1, off11:off11 + 128],
                        in_=st_[:, cbase + half:cbase + half + 1], identity=idf[:]),
                        waits=[tokn, psb_rd[2 + (half % 2)]], inc="st")
                    bct = B.op("vector", lambda e, row=row, half=half, off11=off11: e.tensor_copy(
                        out=row[0:1, half * 128:half * 128 + 128],
                        in_=psB[0:1, off11:off11 + 128]),
                        waits=[t11], inc="sv")
                    bctoks.append(bct)
                    psb_rd[2 + (half % 2)] = bct
            s2bc = arnG[:, 772:1028]
            rs2bc = arnG[:, 1028:1284]
            t12 = B.op("tensor", lambda e, srow=srow: e.matmul(
                out=psB[:, 1024:1280], lhsT=onesf[0:1, 0:128], rhs=srow[0:1, 0:256],
                start=True, stop=True), waits=bctoks + [psb_rd[2]], inc="st")
            t13 = B.op("tensor", lambda e, rrow=rrow: e.matmul(
                out=psB[:, 1536:1792], lhsT=onesf[0:1, 0:128], rhs=rrow[0:1, 0:256],
                start=True, stop=True), waits=bctoks + [psb_rd[3]], inc="st")
            tbc_s = B.op("vector", lambda e, s2bc=s2bc: e.tensor_copy(
                out=s2bc, in_=psB[:, 1024:1280]), waits=[t12], inc="sv")
            tbc_r = B.op("vector", lambda e, rs2bc=rs2bc: e.tensor_copy(
                out=rs2bc, in_=psB[:, 1536:1792]), waits=[t13], inc="sv")
            psb_rd[2] = tbc_s
            psb_rd[3] = tbc_r

            y2_last = None
            tg2_prev = None
            tg3_prev = None
            for hc in range(NC_H):
                slot = hc % 4
                gqt = arnG[:, 1284:1540]
                gqi = arnG[:, 1540:1796].bitcast(I32)[:, 0:256]
                gqb = arnP[:, slot * 256:slot * 256 + 256]
                tg1 = B.op("vector", lambda e, hc=hc, gqt=gqt: e.tensor_tensor(
                    out=gqt, in0=g_f[:, hc * 256:hc * 256 + 256], in1=s2bc,
                    op=ALU.mult), waits=[tbc_s, conv_done[hc]], inc="sv")
                tg2 = B.op("vector", lambda e, gqt=gqt, gqi=gqi: e.tensor_copy(
                    out=gqi, in_=gqt), waits=[tg1, tg3_prev], inc="sv")
                tg2_prev = tg2
                tg3 = B.op("vector", lambda e, gqi=gqi, gqb=gqb: e.tensor_copy(
                    out=gqb, in_=gqi),
                    waits=[tg2, gqb_user.get((hc - 2) % 4)], inc="sv")
                if el == 0 and tau == 0 and hc == 1 and "gq" in dbg_t:
                    gq_dump_tok = B.dma("scalar", dbg_t["gq"][0:128, :],
                                        arnP[:, 0:512], waits=[tg3])
                tg3_prev = tg3
                tw2 = B.dma("sync", wsb2[:, slot * 1024:slot * 1024 + 1024],
                            scrw2[hc * 128:hc * 128 + 128, :],
                            waits=[w2slot_user.get((hc - 2) % 4),
                                   (wo_done if tau == 0 and hc < 4 else None)])
                lt = None
                for oc in range(NC_E):
                    w = [tg3, tw2]
                    if hc == 0:
                        w += y2_evac_prev + [B.tok("sv"), y1_evac.get(0), y1_evac.get(1)]
                    lt = B.op("tensor", lambda e, oc=oc, gqb=gqb, slot=slot, hc=hc: e.matmul(
                        out=ybank(oc),
                        lhsT=wsb2[:, slot * 1024 + oc * 128:slot * 1024 + oc * 128 + 128],
                        rhs=gqb, start=(hc == 0), stop=(hc == NC_H - 1),
                        skip_group_check=True), waits=w, inc="st")
                gqb_user[slot] = lt
                w2slot_user[slot] = lt
                y2_last = lt
            if el == 0 and tau == 0 and "stats2" in dbg_t:
                dbg_gate.append(B.dma("scalar", dbg_t["stats2"][0:128, :],
                                      arnS[:, :], waits=[B.tok("sv")]))
            y2full = arnG[:, 2048:4096]
            y2_evac_prev = []
            for oc in range(NC_E):
                y2_evac_prev.append(B.op("vector", lambda e, oc=oc, y2full=y2full: e.tensor_tensor(
                    out=y2full[:, oc * 256:oc * 256 + 256], in0=ybank(oc), in1=rs2bc,
                    op=ALU.mult), waits=[y2_last, tbc_r], inc="sv"))
            if el == 0 and tau == 0 and "misc" in dbg_t:
                mb = arnH[:, 0:1024]
                mc1 = B.op("vector", lambda e, mb=mb: e.tensor_copy(
                    out=mb[:, 0:256], in_=s2bc), waits=[tbc_s], inc="sv")
                mc2 = B.op("vector", lambda e, mb=mb: e.tensor_copy(
                    out=mb[:, 256:512], in_=rs2bc), waits=[tbc_r, mc1], inc="sv")
                gq0 = arnG[:, 1796:1924].bitcast(BF16)
                mc3 = B.op("vector", lambda e, mb=mb, gq0=gq0: e.tensor_copy(
                    out=mb[:, 512:768], in_=gq0), waits=[gq_dbg_tok, mc2], inc="sv")
                mdt = B.dma("scalar", dbg_t["misc"][0:128, :], mb, waits=[mc3])
                dbg_gate.append(mdt)
                out_toks[0] = mdt
            if el == 0 and tau == 0 and "y2" in dbg_t:
                dbg_gate.append(B.dma("scalar", dbg_t["y2"][0:128, :],
                                      y2full, waits=y2_evac_prev))
            for half in range(2):
                y2tm = arnH[:, half * 1024:half * 1024 + 1024]
                xres = arnH[:, 2048:3072]
                rows0 = tau * 256 + half * 128
                cps = []
                for oc in range(NC_E):
                    slot = oc % 2
                    pv = psB[:, 1024 + slot * 512:1024 + slot * 512 + 128]
                    w = list(y2_evac_prev) + [psb_rd[2 + slot]]
                    if tau == 0 and half == 0 and oc == 0:
                        w.append(B.tok("sd"))
                    t14 = B.op("tensor", lambda e, oc=oc, half=half, pv=pv, y2full=y2full: e.transpose(
                        out=pv, in_=y2full[:, oc * 256 + half * 128:oc * 256 + half * 128 + 128],
                        identity=idf[:]), waits=w, inc="st")
                    cp = B.op("vector", lambda e, oc=oc, pv=pv, y2tm=y2tm: e.tensor_copy(
                        out=y2tm[:, oc * 128:oc * 128 + 128], in_=pv),
                        waits=[t14, out_toks.get(half)], inc="sv")
                    cps.append(cp)
                    psb_rd[2 + slot] = cp
                txr = B.dma("sync", xres, scrx[el, rows0:rows0 + 128, :],
                            waits=[spills[rows0 // 128], B.tok("sv")])
                ta3 = B.op("vector", lambda e, y2tm=y2tm, xres=xres: e.tensor_tensor(
                    out=y2tm, in0=y2tm, in1=xres, op=ALU.add), waits=[txr] + cps, inc="sv")
                x2b = arnG[:, 256:1280]
                ta4 = B.op("vector", lambda e, y2tm=y2tm, x2b=x2b: e.tensor_tensor(
                    out=x2b, in0=y2tm, in1=y2tm, op=ALU.mult),
                    waits=[ta3], inc="sv")
                g1 = B.op("vector", lambda e, y2tm=y2tm: e.tensor_reduce(
                    out=st_[:, 80:81], in_=y2tm, axis=AX.X, op=ALU.add),
                    waits=[ta3], inc="sv")
                g2 = B.op("vector", lambda e, x2b=x2b: e.tensor_reduce(
                    out=st_[:, 81:82], in_=x2b, axis=AX.X, op=ALU.add),
                    waits=[ta4], inc="sv")
                g3 = B.op("vector", lambda e: e.tensor_scalar(out=st_[:, 82:83],
                     in0=st_[:, 80:81], scalar1=1.0 / E, scalar2=None,
                     op0=ALU.mult), waits=[g1], inc="sv")
                g4 = B.op("vector", lambda e: e.tensor_scalar(out=st_[:, 83:84],
                     in0=st_[:, 81:82], scalar1=1.0 / E, scalar2=None,
                     op0=ALU.mult), waits=[g2], inc="sv")
                g5 = B.op("vector", lambda e: e.tensor_tensor(out=st_[:, 84:85],
                     in0=st_[:, 82:83], in1=st_[:, 82:83], op=ALU.mult),
                     waits=[g3], inc="sv")
                g6 = B.op("vector", lambda e: e.tensor_tensor(out=st_[:, 85:86],
                     in0=st_[:, 83:84], in1=st_[:, 84:85], op=ALU.subtract),
                     waits=[g4, g5], inc="sv")
                tve2 = B.op("vector", lambda e: e.tensor_scalar(out=st_[:, 86:87],
                            in0=st_[:, 85:86], scalar1=1e-5, scalar2=None,
                            op0=ALU.add), waits=[g6], inc="sv")
                rstd2 = rsqrt_chain(st_[:, 86:87], st_[:, 91:92],
                                    [st_[:, 87:88], st_[:, 88:89], st_[:, 89:90]],
                                    [tve2])
                tln2 = B.op("vector", lambda e, y2tm=y2tm: e.tensor_scalar(
                    out=y2tm, in0=y2tm, scalar1=st_[:, 82:83], scalar2=st_[:, 91:92],
                    op0=ALU.subtract, op1=ALU.mult), waits=[rstd2], inc="sv")
                out_toks[half] = B.dma("scalar", yout[el, rows0:rows0 + 128, :],
                                       y2tm, waits=[tln2])

    B.op("scalar", None, waits=[B.tok("sq")])
    B.finalize()
    return nc


_IDB = np.eye(128, dtype=ml_dtypes.bfloat16)
_IDF = np.eye(128, dtype=np.float32)
_ONES = np.ones((128, 128), dtype=np.float32)


def core_inputs(L0_attn_x, L0_attn_Wk, L0_attn_Wq, L0_attn_Wo, L0_attn_bo,
                L0_ffn_w1, L0_ffn_conv_w, L0_ffn_conv_b, L0_ffn_w2):
    x = np.ascontiguousarray(np.asarray(L0_attn_x, dtype=np.float32))
    com = dict(
        wq=np.ascontiguousarray(np.asarray(L0_attn_Wq, dtype=np.float32)),
        wk=np.ascontiguousarray(np.asarray(L0_attn_Wk, dtype=np.float32)),
        wo=np.ascontiguousarray(np.asarray(L0_attn_Wo, dtype=np.float32)),
        bo=np.ascontiguousarray(np.asarray(L0_attn_bo, dtype=np.float32)),
        w1=np.ascontiguousarray(np.asarray(L0_ffn_w1, dtype=np.float32)),
        cw=np.ascontiguousarray(np.asarray(L0_ffn_conv_w, dtype=np.float32).reshape(DH, 3)),
        cb=np.ascontiguousarray(np.asarray(L0_ffn_conv_b, dtype=np.float32)),
        w2=np.ascontiguousarray(np.asarray(L0_ffn_w2, dtype=np.float32)),
        idb=_IDB, idf=_IDF, ones=_ONES)
    return [dict(x=np.ascontiguousarray(x[i * BEL:(i + 1) * BEL]), **com)
            for i in range(NCORES)]


def kernel(L0_attn_x, L0_attn_Wk, L0_attn_Wq, L0_attn_Wo, L0_attn_bo,
           L0_ffn_w1, L0_ffn_conv_w, L0_ffn_conv_b, L0_ffn_w2):
    in_maps = core_inputs(L0_attn_x, L0_attn_Wk, L0_attn_Wq, L0_attn_Wo,
                          L0_attn_bo, L0_ffn_w1, L0_ffn_conv_w, L0_ffn_conv_b,
                          L0_ffn_w2)
    nc = build_nc()
    res = run_bass_kernel_spmd(nc, in_maps, core_ids=list(range(NCORES)))
    return np.concatenate([np.asarray(r["y"], dtype=np.float32)
                           for r in res.results], axis=0)


# version 08
# speedup vs baseline: 1.3130x; 1.3130x over previous
import os
import numpy as np
import ml_dtypes
import concourse.bass as bass
import concourse.mybir as mybir
from concourse.bass_utils import run_bass_kernel_spmd

F32 = mybir.dt.float32
BF16 = mybir.dt.bfloat16
I32 = mybir.dt.int32
AF = mybir.ActivationFunctionType
ALU = mybir.AluOpType
AX = mybir.AxisListType

NCORES = 8
BEL = 2
T = 2048
E = 1024
NH = 16
DH = 4096
TT = 256
NTILE = T // TT
NC_E = E // 128
NC_H = DH // 128
NKT = T // 128
QSCALE = 1.0 / 32.0

ENGS = ("sync", "tensor", "vector", "scalar", "gpsimd")


class Bld:
    def __init__(self, nc):
        self.nc = nc
        names = ["sd", "sq", "sg", "st", "sv", "ss", "sp",
                 "sc", "sw", "sws", "sxb", "sxq", "ssp", "sxl", "sxr",
                 "sw1", "sw2"]
        self.sem = {n: nc.alloc_semaphore(name=n) for n in names}
        self.cnt = {n: 0 for n in names}
        self.ops = {e: [] for e in ENGS}
        self.water = {e: {n: 0 for n in names} for e in ENGS}
        self.own = {"tensor": "st", "vector": "sv", "scalar": "ss", "gpsimd": "sp"}
        self.drain_mark = {e: 0 for e in ENGS}

    def tok(self, sem):
        return (sem, self.cnt[sem])

    def barrier(self):
        return [(n, self.cnt[n]) for n in self.cnt]

    def op(self, eng, fn, waits=(), inc=None, inc_n=1):
        own = self.own.get(eng)
        if inc is None and own is not None and fn is not None:
            inc = own
        w = []
        need_drain = False
        for t in waits:
            if t is None:
                continue
            s, v = t
            if v <= 0:
                continue
            if s == own:
                if v > self.drain_mark[eng]:
                    need_drain = True
                continue
            if v > self.water[eng][s]:
                w.append((s, v))
                self.water[eng][s] = v
        if need_drain:
            self.ops[eng].append(([], lambda e: e.drain(), None, 0))
            self.drain_mark[eng] = self.cnt[own]
        tok = None
        if inc is not None:
            self.cnt[inc] += inc_n
            tok = (inc, self.cnt[inc])
        self.ops[eng].append((w, fn, inc, inc_n))
        return tok

    def dma(self, queue, out, in_, waits=(), transpose=False, slow=False,
            sem=None, chain=False):
        if sem is None:
            sem = {"sync": "sd", "scalar": "sq", "gpsimd": "sg"}[queue]
        if transpose:
            fn = lambda e, o=out, i=in_: e.dma_start_transpose(out=o, in_=i)
        elif slow:
            fn = lambda e, o=out, i=in_: e.dma_start(
                out=o, in_=i, allow_slow_non_contiguous=True)
        else:
            fn = lambda e, o=out, i=in_: e.dma_start(out=o, in_=i)
        w = list(waits)
        if chain and getattr(self, "last_dma", {}).get(sem):
            w.append(self.last_dma[sem])
        if not hasattr(self, "last_dma"):
            self.last_dma = {}
        tok = self.op(queue, fn, waits=w, inc=sem, inc_n=16)
        self.last_dma[sem] = tok
        return tok

    def finalize(self):
        sem = self.sem
        with self.nc.Block() as block:
            def mk(name):
                def body(eng):
                    for (w, fn, inc, n) in self.ops[name]:
                        for (s, v) in w:
                            eng.wait_ge(sem[s], v)
                        if fn is None:
                            continue
                        inst = fn(eng)
                        if inc is not None and inst is not None:
                            inst.then_inc(sem[inc], n)
                return body
            block.sync(mk("sync"))
            block.tensor(mk("tensor"))
            block.vector(mk("vector"))
            block.scalar(mk("scalar"))
            block.gpsimd(mk("gpsimd"))


def build_nc(dbg=(), phases="full"):
    WLVL = int(os.environ.get("WLVL", "12"))
    BLVL = int(os.environ.get("BLVL", "9"))
    BTT = int(os.environ.get("BTT", "8"))
    BCO = int(os.environ.get("BCO", "8"))
    BSB = int(os.environ.get("BSB", "2"))
    nc = bass.Bass("TRN2", target_bir_lowering=False)
    B = Bld(nc)

    xin = nc.dram_tensor("x", [BEL, T, E], F32, kind="ExternalInput").ap()
    wq_ = nc.dram_tensor("wq", [E, E], F32, kind="ExternalInput").ap()
    wk_ = nc.dram_tensor("wk", [E, E], F32, kind="ExternalInput").ap()
    wo_ = nc.dram_tensor("wo", [E, E], F32, kind="ExternalInput").ap()
    bo_ = nc.dram_tensor("bo", [E], F32, kind="ExternalInput").ap()
    w1_ = nc.dram_tensor("w1", [DH, E], F32, kind="ExternalInput").ap()
    cw_ = nc.dram_tensor("cw", [DH, 3], F32, kind="ExternalInput").ap()
    cb_ = nc.dram_tensor("cb", [DH], F32, kind="ExternalInput").ap()
    w2_ = nc.dram_tensor("w2", [E, DH], F32, kind="ExternalInput").ap()
    idb_ = nc.dram_tensor("idb", [128, 128], BF16, kind="ExternalInput").ap()
    idf_ = nc.dram_tensor("idf", [128, 128], F32, kind="ExternalInput").ap()
    ones_ = nc.dram_tensor("ones", [128, 128], F32, kind="ExternalInput").ap()
    yout = nc.dram_tensor("y", [BEL, T, E], F32, kind="ExternalOutput").ap()
    scrw1 = nc.dram_tensor("scrw1", [E, DH], BF16, kind="Internal").ap()
    scrw2 = nc.dram_tensor("scrw2", [DH, E], BF16, kind="Internal").ap()
    scrx = nc.dram_tensor("scrx", [BEL, T, E], F32, kind="Internal").ap()
    dbg_t = {}
    for (name, shape, dt_) in dbg:
        dbg_t[name] = nc.dram_tensor(name, shape, dt_, kind="ExternalOutput").ap()

    def sb(name, shape, dtype, **kw):
        return nc.alloc_sbuf_tensor(name, shape, dtype, **kw).ap()

    idb = sb("idb_s", [128, 128], BF16)
    idf = sb("idf_s", [128, 128], F32)
    onesf = sb("ones_s", [128, 128], F32)
    bo_sb = sb("bo_s", [128, 8], F32)
    cw_sb = sb("cw_s", [128, 96], F32)
    cb_sb = sb("cb_s", [128, 32], F32)
    wA = sb("wA", [128, 8192], BF16)
    wB = sb("wB", [128, 8192], BF16)
    arnQ = sb("arnQ", [128, 16384], BF16, is_transpose_dest=True)
    arnK = sb("arnK", [128, 16384], BF16)
    arnV = sb("arnV", [128, 16768], BF16)
    arnX = sb("arnX", [128, 16384], BF16, is_transpose_dest=True)
    arnG = sb("arnG", [128, 4096], F32)
    arnH = sb("arnH", [128, 3072], F32)
    arnS = sb("arnS", [128, 1024], F32)
    arnP = sb("arnP", [128, 2048], BF16)
    trst = sb("trst", [128, 4096], BF16)
    wsb = wA[:, 0:4096]
    wsb2 = wA[:, 4096:8192]

    psA = nc.alloc_psum_tensor("psA", [128, 2048], F32).ap()
    psB = nc.alloc_psum_tensor("psB", [128, 2048], F32).ap()

    g_f = arnK.bitcast(F32)
    h_f = arnV.bitcast(F32)
    v_tm = arnV
    x_tmbf = arnV

    def rsqrt_chain(vcol, scol, tmpc, waits):
        t0 = B.op("scalar", lambda e: e.activation(out=tmpc[0], in_=vcol,
                  func=AF.Sqrt), waits=waits, inc="ss")
        a1 = B.op("vector", lambda e: e.reciprocal(out=tmpc[1], in_=tmpc[0]),
                  waits=[t0], inc="sv")
        a2 = B.op("vector", lambda e: e.tensor_tensor(out=tmpc[2], in0=vcol,
                  in1=tmpc[1], op=ALU.mult), waits=[a1], inc="sv")
        a3 = B.op("vector", lambda e: e.tensor_tensor(out=tmpc[1], in0=tmpc[0],
                  in1=tmpc[2], op=ALU.add), waits=[a2], inc="sv")
        a4 = B.op("vector", lambda e: e.tensor_scalar(out=tmpc[0], in0=tmpc[1],
                  scalar1=0.5, scalar2=None, op0=ALU.mult), waits=[a3], inc="sv")
        tk = B.op("vector", lambda e: e.reciprocal(out=scol, in_=tmpc[0]),
                  waits=[a4], inc="sv")
        return tk

    B.dma("sync", idb[:], idb_[:], sem="sc", chain=True)
    B.dma("sync", idf[:], idf_[:], sem="sc", chain=True)
    B.dma("sync", onesf[:], ones_[:], sem="sc", chain=True)
    B.dma("sync", bo_sb[:, 0:8], bo_.rearrange("(c p) -> p c", p=128), slow=True, sem="sc", chain=True)
    B.dma("sync", cw_sb[:, 0:96].rearrange("p (hc t) -> p hc t", t=3),
          cw_.rearrange("(hc p) t -> p hc t", p=128), sem="sc", chain=True)
    const_tok = B.dma("sync", cb_sb[:, 0:32], cb_.rearrange("(hc p) -> p hc", p=128),
                      slow=True, sem="sc", chain=True)
    t_wq = B.dma("gpsimd", wA[:].rearrange("p (a b) -> p a b", a=8), wq_.rearrange("(ci p) o -> p ci o", p=128), chain=True)
    t_wk = B.dma("gpsimd", wB[:].rearrange("p (a b) -> p a b", a=8), wk_.rearrange("(ci p) o -> p ci o", p=128), chain=True)

    wstg = arnG.bitcast(F32)[:, 0:2048]
    wqf = arnG[:, 2048:4096]
    wqi = arnH[:, 0:2048].bitcast(I32)[:, 0:2048]
    wqb_t = arnH[:, 2048:3072].bitcast(BF16)

    def wquant(widx, src_of_q, tile_col, slot_col, dst_q, rr_kw):
        sums = arnS[:, 0:16]
        for qi in range(16):
            tl = B.dma("sync", wstg[:].rearrange("p (a b) -> p a b", a=2),
                       src_of_q(qi), waits=[B.tok("sv")] if qi > 0 else [],
                       sem="sw", chain=True)
            B.op("vector", lambda e, qi=qi: e.tensor_reduce(
                out=sums[:, qi:qi + 1], in_=wstg[:], axis=AX.X, op=ALU.add,
                apply_absolute_value=True), waits=[tl], inc="sv")
        tv = B.tok("sv")
        t1 = B.op("tensor", lambda e: e.matmul(
            out=psB[0:1, 0:16], lhsT=onesf[:, 0:1], rhs=sums[:, 0:16],
            start=True, stop=True), waits=[tv, const_tok], inc="st")
        c0 = 16 + 12 * widx
        tot, mean, meanc = arnS[0:1, c0:c0 + 1], arnS[0:1, c0 + 1:c0 + 2], arnS[0:1, c0 + 2:c0 + 3]
        scale, scale_r = arnS[0:1, c0 + 3:c0 + 4], arnS[0:1, c0 + 4:c0 + 5]
        b1 = B.op("vector", lambda e: e.tensor_reduce(out=tot, in_=psB[0:1, 0:16],
                  axis=AX.X, op=ALU.add), waits=[t1], inc="sv")
        b2 = B.op("vector", lambda e: e.tensor_scalar(out=mean, in0=tot,
                  scalar1=1.0 / (DH * E), scalar2=None, op0=ALU.mult),
                  waits=[b1], inc="sv")
        b3 = B.op("vector", lambda e: e.tensor_scalar(out=meanc, in0=mean,
                  scalar1=1e-5, scalar2=None, op0=ALU.max), waits=[b2], inc="sv")
        b4 = B.op("vector", lambda e: e.reciprocal(out=scale, in_=meanc),
                  waits=[b3], inc="sv")
        tvr = B.op("vector", lambda e: e.reciprocal(out=scale_r, in_=scale),
                   waits=[b4], inc="sv")
        t2 = B.op("tensor", lambda e: e.matmul(
            out=psB[:, 32:33], lhsT=onesf[0:1, 0:128], rhs=scale,
            start=True, stop=True), waits=[tvr], inc="st")
        t3 = B.op("tensor", lambda e: e.matmul(
            out=psB[:, 552:553], lhsT=onesf[0:1, 0:128], rhs=scale_r,
            start=True, stop=True), waits=[tvr], inc="st")
        scol, srcol = arnS[:, c0 + 5:c0 + 6], arnS[:, c0 + 6:c0 + 7]
        B.op("vector", lambda e: e.tensor_copy(out=scol, in_=psB[:, 32:33]),
             waits=[t2], inc="sv")
        tbc = B.op("vector", lambda e: e.tensor_copy(out=srcol, in_=psB[:, 552:553]),
                   waits=[t3], inc="sv")
        tev = {}
        dtok = {}
        last_tt = None
        for qi in range(16):
            tslot = qi % 2
            tl = B.dma("sync", wstg[:].rearrange("p (a b) -> p a b", a=2),
                       src_of_q(qi), waits=[tbc, B.tok("sv")],
                       sem="sw", chain=True)
            tm1 = B.op("vector", lambda e: e.tensor_scalar(
                out=wqf[:], in0=wstg[:], scalar1=scol, scalar2=None,
                op0=ALU.mult), waits=[tl, tbc], inc="sv")
            tm2 = B.op("vector", lambda e: e.tensor_copy(out=wqi[:], in_=wqf[:]),
                       waits=[tm1], inc="sv")
            tm3 = B.op("vector", lambda e: e.tensor_scalar(
                out=wqi[:], in0=wqi[:], scalar1=1, scalar2=None,
                op0=ALU.min), waits=[tm2], inc="sv")
            tm4 = B.op("vector", lambda e: e.tensor_scalar(
                out=wqi[:], in0=wqi[:], scalar1=-1, scalar2=None,
                op0=ALU.max), waits=[tm3], inc="sv")
            tq = B.op("vector", lambda e: e.tensor_copy(
                out=wqb_t[:], in_=wqi[:]),
                waits=[tm4] + ([last_tt] if last_tt else []), inc="sv")
            for k in range(16):
                slot = k % 2
                cc = tile_col(k)
                pv = psB[:, 1024 + slot * 512:1024 + slot * 512 + 64].bitcast(BF16)
                w = [tq]
                if (qi, k - 2) in tev:
                    w.append(tev[(qi, k - 2)])
                elif qi > 0 and k < 2:
                    w.append(tev[(qi - 1, 14 + k)])
                last_tt = B.op("tensor", lambda e, cc=cc, pv=pv: e.transpose(
                    out=pv, in_=wqb_t[:, cc:cc + 128],
                    identity=idb[:]), waits=w, inc="st")
                we = [last_tt, dtok.get(qi - 2)]
                sc_ = slot_col(k)
                tev[(qi, k)] = B.op("vector", lambda e, tslot=tslot, sc_=sc_, pv=pv: e.tensor_copy(
                    out=trst[:, tslot * 2048 + sc_:tslot * 2048 + sc_ + 128],
                    in_=pv), waits=we, inc="sv")
            dtok[qi] = B.dma("sync", dst_q(qi),
                             trst[:, tslot * 2048:tslot * 2048 + 2048].rearrange(
                                 "p (a b) -> p a b", **rr_kw),
                             waits=[tev[(qi, 15)]], sem="sws", chain=True)

    wquant(0,
           lambda qi: w1_[qi * 256:(qi + 1) * 256, :].rearrange("(j p) d -> p j d", p=128),
           lambda k: (k // 8) * 1024 + (k % 8) * 128,
           lambda k: (k % 8) * 256 + (k // 8) * 128,
           lambda qi: scrw1[0:1024, qi * 256:(qi + 1) * 256].rearrange(
               "(dc p) jc -> p dc jc", p=128),
           dict(a=8))
    wquant(1,
           lambda qi: w2_[:, qi * 256:(qi + 1) * 256].rearrange("(j p) h -> p j h", p=128),
           lambda k: (k // 2) * 256 + (k % 2) * 128,
           lambda k: (k % 2) * 1024 + (k // 2) * 128,
           lambda qi: scrw2[qi * 256:(qi + 1) * 256, 0:1024].rearrange(
               "(h2 p) jc -> p h2 jc", p=128),
           dict(a=2))
    w1sr_col = arnS[:, 22:23]
    w2sr_col = arnS[:, 34:35]
    B.op("vector", lambda e: e.tensor_scalar(
        out=cw_sb[:, 0:96], in0=cw_sb[:, 0:96], scalar1=w1sr_col, scalar2=None,
        op0=ALU.mult), waits=[const_tok, B.tok("sv")], inc="sv")

    w_ps_tok = B.tok("st")
    w_sv_tok = B.tok("sv")
    w_sd_tok = B.tok("sws")
    out_toks = {}
    psb_rd = {2: w_sv_tok, 3: w_sv_tok}
    nel = 0 if phases == "w" else (1 if phases != "full" else BEL)
    for el in range(nel):
        bar = B.barrier()

        tcast = []
        for tc in range(NKT):
            tcast.append(B.dma("gpsimd", x_tmbf[:, tc * 1024:(tc + 1) * 1024],
                               xin[el, tc * 128:(tc + 1) * 128, :], waits=bar,
                               chain=True))
        for tc in range(NKT):
            for ci in range(NC_E):
                xbf_tok = B.dma("sync",
                                arnX[:, ci * 2048 + tc * 128:ci * 2048 + tc * 128 + 128],
                                x_tmbf[:, tc * 1024 + ci * 128:tc * 1024 + ci * 128 + 128],
                                waits=bar + [tcast[tc]], transpose=True, sem="sxb")

        if phases == "a0":
            continue
        if el > 0:
            t_wq = B.dma("gpsimd", wA[:].rearrange("p (a b) -> p a b", a=8),
                         wq_.rearrange("(ci p) o -> p ci o", p=128), waits=bar,
                         chain=True)
        evac_glob = {}
        q_last_mm = None
        for wi, (wten, dst, scl, wtok) in enumerate(
                [(wA, arnQ, QSCALE, t_wq), (wB, arnK, 1.0, t_wk)]):
            for co in range(NC_E):
                gidx = wi * NC_E + co
                ps = psA
                mmw = [xbf_tok, wtok, evac_glob.get(gidx - 1)] + bar
                lt = None
                for ci in range(NC_E):
                    for seg in range(4):
                        lt = B.op("tensor", lambda e, wten=wten, ci=ci, co=co, seg=seg, ps=ps: e.matmul(
                            out=ps[:, seg * 512:(seg + 1) * 512],
                            lhsT=wten[:, ci * 1024 + co * 128:ci * 1024 + co * 128 + 128],
                            rhs=arnX[:, ci * 2048 + seg * 512:ci * 2048 + (seg + 1) * 512],
                            start=(ci == 0), stop=(ci == NC_E - 1),
                            skip_group_check=True), waits=mmw, inc="st")
                evac_glob[gidx] = B.op("scalar", lambda e, dst=dst, co=co, scl=scl, ps=ps: e.activation(
                    out=dst[:, co * 2048:(co + 1) * 2048], in_=ps[:, 0:2048],
                    func=AF.Identity, scale=scl), waits=[lt], inc="ss")
            if wi == 0:
                q_last_mm = lt
        qk_tok = B.tok("ss")
        t_wo = B.dma("gpsimd", wA[:].rearrange("p (a b) -> p a b", a=8),
                     wo_.rearrange("(ci p) o -> p ci o", p=128),
                     waits=[q_last_mm], chain=True)

        B.op("vector", lambda e: e.tensor_reduce(out=arnS[:, 40:41], in_=arnQ[:],
             axis=AX.X, op=ALU.max, apply_absolute_value=True),
             waits=[qk_tok], inc="sv")
        tmk = B.op("vector", lambda e: e.tensor_reduce(out=arnS[:, 41:42], in_=arnK[:],
                   axis=AX.X, op=ALU.max, apply_absolute_value=True),
                   waits=[qk_tok], inc="sv")
        t4 = B.op("tensor", lambda e: e.transpose(out=psA[0:2, 0:128],
                  in_=arnS[:, 40:42], identity=idf[:]), waits=[tmk], inc="st")
        t5 = B.op("vector", lambda e: e.tensor_reduce(out=arnS[0:2, 43:44],
                  in_=psA[0:2, 0:128], axis=AX.X, op=ALU.max),
                  waits=[t4], inc="sv")
        t6 = B.op("tensor", lambda e: e.transpose(out=psA[0:1, 128:130],
                  in_=arnS[0:2, 43:44], identity=idf[0:2, 0:2]), waits=[t5], inc="st")
        c1_ = B.op("vector", lambda e: e.tensor_copy(out=arnS[0:1, 47:49],
                   in_=psA[0:1, 128:130]), waits=[t6], inc="sv")
        c2_ = B.op("vector", lambda e: e.tensor_tensor(out=arnS[0:1, 44:45],
                   in0=arnS[0:1, 47:48], in1=arnS[0:1, 48:49], op=ALU.mult),
                   waits=[c1_], inc="sv")
        t7 = B.op("vector", lambda e: e.tensor_scalar(out=arnS[0:1, 45:46],
                  in0=arnS[0:1, 44:45], scalar1=-64.0, scalar2=None,
                  op0=ALU.mult), waits=[c2_], inc="sv")
        t8 = B.op("tensor", lambda e: e.matmul(out=psA[:, 130:131],
                  lhsT=onesf[0:1, 0:128], rhs=arnS[0:1, 45:46],
                  start=True, stop=True), waits=[t7], inc="st")
        negB_col = arnS[:, 46:47]
        negB_tok = B.op("vector", lambda e: e.tensor_copy(out=negB_col,
                        in_=psA[:, 130:131]), waits=[t8], inc="sv")

        if phases == "a1":
            continue
        v4 = v_tm[:, 0:16640].rearrange("p (kt h d) -> p kt h d", kt=NKT, h=NH)
        tms0 = B.op("vector", lambda e: e.memset(v_tm[:, 16640:16768], 0.0),
                    waits=[B.tok("sxb"), qk_tok, w_ps_tok, w_sv_tok], inc="sv")
        tms = B.op("vector", lambda e: e.memset(v4[:, :, :, 64:65], 1.0),
                   waits=[B.tok("sxb"), qk_tok, tms0], inc="sv")
        vt_ev = {}
        k2 = 0
        for ci in range(NC_E):
            for kt in range(NKT):
                slot = k2 % 2
                pv = psB[:, 1024 + slot * 512:1024 + slot * 512 + 64].bitcast(BF16)
                w = [qk_tok, tms, negB_tok, vt_ev.get(k2 - 2)]
                tt_ = B.op("tensor", lambda e, ci=ci, kt=kt, pv=pv: e.transpose(
                    out=pv, in_=arnK[:, ci * 2048 + kt * 128:ci * 2048 + kt * 128 + 128],
                    identity=idb[:]), waits=w, inc="st")
                B.op("vector", lambda e, ci=ci, kt=kt, pv=pv: e.tensor_copy(
                    out=v_tm[:, kt * 1040 + (2 * ci) * 65:kt * 1040 + (2 * ci) * 65 + 64],
                    in_=pv[:, 0:64]), waits=[tt_], inc="sv")
                vt_ev[k2] = B.op("vector", lambda e, ci=ci, kt=kt, pv=pv: e.tensor_copy(
                    out=v_tm[:, kt * 1040 + (2 * ci + 1) * 65:kt * 1040 + (2 * ci + 1) * 65 + 64],
                    in_=pv[:, 64:128]), waits=[tt_], inc="sv")
                k2 += 1
        vtm_tok = vt_ev[k2 - 1]

        if phases == "a2":
            continue
        exp_hist = []
        pv_hist = []
        tail_sv = None
        tail_sg = None
        tail_te = None
        tail_tr = None
        tail_tbc2 = None
        attn_tails = []
        steps = [(h, qh, kt) for h in range(NH) for qh in range(2)
                 for kt in range(NKT)]
        nst = len(steps)

        def emit_scores(i):
            h, qh, kt = steps[i]
            ci, off = h // 2, 64 * (h % 2)
            ps = psA[:, (i % 2) * 1024:(i % 2) * 1024 + 1024]
            sw = [qk_tok, negB_tok, vtm_tok]
            if len(exp_hist) >= 1:
                sw.append(exp_hist[-1])
            lt = None
            for seg in range(2):
                lt = B.op("tensor", lambda e, ci=ci, off=off, kt=kt, qh=qh, seg=seg, ps=ps: e.matmul(
                    out=ps[:, seg * 512:(seg + 1) * 512],
                    lhsT=arnK[off:off + 64, ci * 2048 + kt * 128:ci * 2048 + kt * 128 + 128],
                    rhs=arnQ[off:off + 64, ci * 2048 + qh * 1024 + seg * 512:ci * 2048 + qh * 1024 + (seg + 1) * 512],
                    start=True, stop=True, tile_position=(off, 0),
                    skip_group_check=True), waits=sw, inc="st")
            return lt

        score_tok = {}

        def emit_exp_pv(i):
            nonlocal tail_sv, tail_sg, tail_te, tail_tr, tail_tbc2
            h, qh, kt = steps[i]
            ps = psA[:, (i % 2) * 1024:(i % 2) * 1024 + 1024]
            ptb = arnP[:, (i % 2) * 1024:(i % 2) * 1024 + 1024]
            ew = [score_tok[i], negB_tok]
            if len(pv_hist) >= 2:
                ew.append(pv_hist[-2])
            et = B.op("scalar", lambda e, ptb=ptb, ps=ps: e.activation(
                out=ptb, in_=ps, func=AF.Exp, bias=negB_col, scale=1.0),
                waits=ew, inc="ss")
            exp_hist.append(et)
            pw = [et]
            if kt == 0 and qh == 0:
                pw += [vtm_tok, tail_te, tail_tr]
                if h == 0 and el == 0:
                    pw += [w_ps_tok, w_sv_tok]
            if kt == 0 and qh == 1:
                pw.append(tail_tbc2)
                if h == 0 and el == 0:
                    pw += [w_ps_tok, w_sv_tok]
            pv_last = None
            for seg in range(2):
                pv_last = B.op("tensor", lambda e, h=h, kt=kt, qh=qh, seg=seg, ptb=ptb: e.matmul(
                    out=psB[0:128, qh * 1024 + seg * 512:qh * 1024 + (seg + 1) * 512],
                    lhsT=v_tm[:, kt * 1040 + h * 65:kt * 1040 + h * 65 + 128],
                    rhs=ptb[:, seg * 512:(seg + 1) * 512],
                    start=(kt == 0), stop=(kt == NKT - 1),
                    skip_group_check=True), waits=pw, inc="st")
            pv_hist.append(pv_last)
            if not (qh == 1 and kt == NKT - 1):
                return
            ci, off = h // 2, 64 * (h % 2)
            tmp = arnG[0:64, 0:2048]
            r_row = arnG[64:65, 0:2048]
            r_bc = arnG[0:64, 2048:4096]
            te = B.op("scalar", lambda e: e.activation(
                out=tmp, in_=psB[0:64, 0:2048], func=AF.Identity),
                waits=[pv_last, tail_sv, tail_sg, w_sv_tok], inc="ss")
            tr = B.op("vector", lambda e: e.reciprocal(
                out=r_row, in_=psB[64:65, 0:2048]), waits=[pv_last], inc="sv")
            tbc2 = None
            for rnd in range(2):
                bt = None
                for seg in range(2):
                    gseg = rnd * 2 + seg
                    bt = B.op("tensor", lambda e, seg=seg, gseg=gseg: e.matmul(
                        out=psB[0:64, 1024 + seg * 512:1024 + (seg + 1) * 512],
                        lhsT=onesf[64:65, 0:64],
                        rhs=r_row[0:1, gseg * 512:(gseg + 1) * 512],
                        start=True, stop=True, skip_group_check=True,
                        tile_position=(64, 0)),
                        waits=[te, tr, tbc2], inc="st")
                tbc2 = B.op("vector", lambda e, rnd=rnd: e.tensor_copy(
                    out=r_bc[:, rnd * 1024:(rnd + 1) * 1024],
                    in_=psB[0:64, 1024:2048]),
                    waits=[bt, tail_sg, psb_rd[2], psb_rd[3]], inc="sv")
                psb_rd[2] = tbc2
                psb_rd[3] = tbc2
            tail_sv = B.op("vector", lambda e: e.tensor_tensor(
                out=r_bc, in0=tmp, in1=r_bc, op=ALU.mult), waits=[tbc2], inc="sv")
            tail_sg = B.dma("gpsimd",
                            arnX[off:off + 64, ci * 2048:(ci + 1) * 2048],
                            r_bc, waits=[tail_sv], chain=True)
            tail_te, tail_tr, tail_tbc2 = te, tr, tbc2
            attn_tails.append(tail_sg)

        for gi in range(nst + 1):
            if gi < nst:
                score_tok[gi] = emit_scores(gi)
            if gi >= 1:
                emit_exp_pv(gi - 1)
        attn_tok = attn_tails[-1]
        attn_sv = tail_sv

        if el == 0 and "q" in dbg_t:
            B.dma("scalar", dbg_t["q"][0:128, :], arnQ[:, :], waits=[qk_tok])
            B.dma("scalar", dbg_t["k"][0:128, :], arnK[:, :], waits=[qk_tok])
        if el == 0 and "attn" in dbg_t:
            B.dma("scalar", dbg_t["attn"][0:128, :], arnX[:, :], waits=[attn_tok])

        if phases == "a3":
            continue
        st_ = arnS
        xq_first = True
        spills = {}
        wo_evac = {}
        for tt in range(NTILE):
            def wo_ps(co):
                c0 = (co % 4) * 512 + (co // 4) * 256
                return psA[:, c0:c0 + 256]
            for co in range(NC_E):
                lt = None
                for ci in range(NC_E):
                    w = [attn_tok, t_wo, wo_evac.get(co), wo_evac.get(co - 4), attn_sv]
                    lt = B.op("tensor", lambda e, co=co, ci=ci, tt=tt: e.matmul(
                        out=wo_ps(co),
                        lhsT=wA[:, ci * 1024 + co * 128:ci * 1024 + co * 128 + 128],
                        rhs=arnX[:, ci * 2048 + tt * 256:ci * 2048 + (tt + 1) * 256],
                        start=(ci == 0), stop=(ci == NC_E - 1)),
                        waits=w, inc="st")
                wo_evac[co] = B.op("scalar", lambda e, co=co: e.activation(
                    out=arnG[:, co * 256:co * 256 + 256],
                    in_=wo_ps(co),
                    func=AF.Identity, bias=bo_sb[:, co:co + 1]),
                    waits=[lt], inc="ss")
            if BLVL < 2 or tt >= BTT:
                continue
            for sb2 in range(BSB):
                tb = tt * 2 + sb2
                asm = arnH[:, (tb % 2) * 1024:(tb % 2) * 1024 + 1024]
                xstage = arnH[:, 2048:3072]
                cps = []
                for co in range(min(NC_E, BCO)):
                    slot = co % 2
                    pv = psB[:, 1024 + slot * 512:1024 + slot * 512 + 128]
                    w = [wo_evac[co], psb_rd[2 + slot]]
                    ttr = B.op("tensor", lambda e, co=co, sb2=sb2, pv=pv: e.transpose(
                        out=pv, in_=arnG[:, co * 256 + sb2 * 128:co * 256 + sb2 * 128 + 128],
                        identity=idf[:]), waits=w, inc="st")
                    cw_ = [ttr, spills.get(tb - 2)]
                    if tt == 0 and sb2 == 0 and co == 0:
                        cw_ += [out_toks.get(0), out_toks.get(1)]
                        if el == 0:
                            cw_ += [w_ps_tok, w_sv_tok]
                    cp = B.op("vector", lambda e, co=co, pv=pv, asm=asm: e.tensor_copy(
                        out=asm[:, co * 128:co * 128 + 128], in_=pv),
                        waits=cw_, inc="sv")
                    cps.append(cp)
                    psb_rd[2 + slot] = cp
                if BLVL < 3:
                    continue
                txl = B.dma("sync", xstage, xin[el, tb * 128:(tb + 1) * 128, :],
                            waits=[B.tok("sv")], sem="sxl", chain=True)
                if BCO < NC_E:
                    continue
                ta = B.op("vector", lambda e, asm=asm, xstage=xstage: e.tensor_tensor(
                    out=asm, in0=asm, in1=xstage, op=ALU.add),
                    waits=[txl] + cps, inc="sv")
                if BLVL < 4:
                    continue
                x2 = arnG[:, 2048:3072]
                ta2 = B.op("vector", lambda e, asm=asm, x2=x2: e.tensor_tensor(
                    out=x2, in0=asm, in1=asm, op=ALU.mult), waits=[ta, attn_tok], inc="sv")
                d1 = B.op("vector", lambda e, asm=asm: e.tensor_reduce(
                    out=st_[:, 50:51], in_=asm, axis=AX.X, op=ALU.add),
                    waits=[ta], inc="sv")
                d2 = B.op("vector", lambda e, x2=x2: e.tensor_reduce(
                    out=st_[:, 51:52], in_=x2, axis=AX.X, op=ALU.add),
                    waits=[ta2], inc="sv")
                d3 = B.op("vector", lambda e: e.tensor_scalar(out=st_[:, 52:53],
                     in0=st_[:, 50:51], scalar1=1.0 / E, scalar2=None,
                     op0=ALU.mult), waits=[d1], inc="sv")
                d4 = B.op("vector", lambda e: e.tensor_scalar(out=st_[:, 53:54],
                     in0=st_[:, 51:52], scalar1=1.0 / E, scalar2=None,
                     op0=ALU.mult), waits=[d2], inc="sv")
                d5 = B.op("vector", lambda e: e.tensor_tensor(out=st_[:, 54:55],
                     in0=st_[:, 52:53], in1=st_[:, 52:53], op=ALU.mult),
                     waits=[d3], inc="sv")
                d6 = B.op("vector", lambda e: e.tensor_tensor(out=st_[:, 55:56],
                     in0=st_[:, 53:54], in1=st_[:, 54:55], op=ALU.subtract),
                     waits=[d4, d5], inc="sv")
                tve = B.op("vector", lambda e: e.tensor_scalar(out=st_[:, 56:57],
                           in0=st_[:, 55:56], scalar1=1e-5, scalar2=None,
                           op0=ALU.add), waits=[d6], inc="sv")
                rstd_tok = rsqrt_chain(st_[:, 56:57], st_[:, 61:62],
                                       [st_[:, 57:58], st_[:, 58:59], st_[:, 59:60]],
                                       [tve])
                tln = B.op("vector", lambda e, asm=asm: e.tensor_scalar(
                    out=asm, in0=asm, scalar1=st_[:, 52:53], scalar2=st_[:, 61:62],
                    op0=ALU.subtract, op1=ALU.mult), waits=[rstd_tok], inc="sv")
                if BLVL < 5:
                    continue
                spills[tb] = B.dma("sync", scrx[el, tb * 128:(tb + 1) * 128, :],
                                   asm, waits=[tln], sem="ssp", chain=True)
                e1 = B.op("vector", lambda e, asm=asm: e.tensor_reduce(
                    out=st_[:, 62:63], in_=asm, axis=AX.X, op=ALU.max,
                    apply_absolute_value=True), waits=[tln], inc="sv")
                e2 = B.op("vector", lambda e: e.tensor_scalar(out=st_[:, 63:64],
                     in0=st_[:, 62:63], scalar1=1e-5, scalar2=None,
                     op0=ALU.max), waits=[e1], inc="sv")
                e3 = B.op("vector", lambda e: e.reciprocal(out=st_[:, 64:65],
                     in_=st_[:, 63:64]), waits=[e2], inc="sv")
                tsc = B.op("vector", lambda e: e.tensor_scalar(out=st_[:, 65:66],
                           in0=st_[:, 64:65], scalar1=127.0, scalar2=None,
                           op0=ALU.mult), waits=[e3], inc="sv")
                tinv = B.op("vector", lambda e: e.tensor_scalar(out=st_[:, 66:67],
                            in0=st_[:, 63:64], scalar1=1.0 / 127.0, scalar2=None,
                            op0=ALU.mult), waits=[e2], inc="sv")
                qf = arnG[:, 3072:4096]
                qi = arnG[:, 2048:3072].bitcast(I32)
                tq1 = B.op("vector", lambda e, asm=asm, qf=qf: e.tensor_scalar(
                    out=qf, in0=asm, scalar1=st_[:, 65:66], scalar2=None,
                    op0=ALU.mult), waits=[tsc], inc="sv")
                tq2 = B.op("vector", lambda e, qf=qf, qi=qi: e.tensor_copy(
                    out=qi, in_=qf), waits=[tq1], inc="sv")
                xqs = arnP[:, (tb % 2) * 1024:(tb % 2) * 1024 + 1024]
                tq3 = B.op("vector", lambda e, qi=qi, xqs=xqs: e.tensor_scalar(
                    out=xqs, in0=qi, scalar1=st_[:, 66:67], scalar2=None,
                    op0=ALU.mult), waits=[tq2, tinv], inc="sv")
                if BLVL < 6:
                    continue
                for ci in range(NC_E):
                    w = [tq3]
                    if xq_first:
                        w += [B.tok("st"), B.tok("sg")]
                        xq_first = False
                    xq_tok = B.dma("sync",
                                   arnQ[:, ci * 2048 + tb * 128:ci * 2048 + tb * 128 + 128],
                                   xqs[:, ci * 128:ci * 128 + 128],
                                   waits=w, transpose=True, sem="sxq", chain=True)

        if el == 0 and "w2qt" in dbg_t:
            B.dma("scalar", dbg_t["w2qt"][0:128, :], scrw2[0:128, :],
                  waits=[xq_tok])
        if el == 0 and "w1qt" in dbg_t:
            B.dma("scalar", dbg_t["w1qt"][0:128, :], scrw1[0:128, 0:1024],
                  waits=[xq_tok])
        if el == 0 and "xq" in dbg_t:
            B.dma("scalar", dbg_t["xq"][0:128, :], arnQ[:, :], waits=[xq_tok])
        if el == 0 and "xres" in dbg_t:
            B.dma("scalar", dbg_t["xres"][:, :], scrx[0, :, :],
                  waits=[spills[tb2] for tb2 in range(NKT)])
        if el == 0 and "stats" in dbg_t:
            B.dma("scalar", dbg_t["stats"][0:128, :], arnS[:, :], waits=[B.tok("sv")])
        if phases == "b":
            continue
        wo_done = B.tok("st")
        dbg_gate = []
        w1slot_user = {}
        w2slot_user = {}
        gqb_user = {}
        y1_evac = {}
        y2_evac_prev = []

        def ybank(oc):
            return (psA if oc < 4 else psB)[:, (oc % 4) * 512:(oc % 4) * 512 + 256]

        for tau in range(NTILE):
            jlo = 1 if tau == 0 else 0
            jhi = 257 if tau == NTILE - 1 else 258
            tok0 = tau * 256 - 1
            mp_tok = None
            conv_done = {}
            for hc in range(NC_H):
                slot = hc % 4
                tw1 = B.dma("sync",
                            wsb[:, slot * 1024:slot * 1024 + 1024].rearrange("p (a b) -> p a b", a=8),
                            scrw1[:, hc * 128:hc * 128 + 128].rearrange("(ci p) h -> p ci h", p=128),
                            waits=[w1slot_user.get((hc - 2) % 4),
                                   (wo_done if tau == 0 and hc < 4 else None),
                                   w_sd_tok if el == 0 else None],
                            sem="sw1", chain=True)
                psY1 = psA[:, (hc % 2) * 512:(hc % 2) * 512 + 512]
                mmw = [tw1, xq_tok, y1_evac.get(hc % 2)]
                if tau == 1 and hc == 0:
                    mmw += dbg_gate
                if tau > 0 and hc == 0:
                    mmw += y2_evac_prev
                if tau == 0 and hc == 0:
                    mmw.append(B.tok("ss"))
                lt = None
                for ci in range(NC_E):
                    lt = B.op("tensor", lambda e, ci=ci, psY1=psY1, jlo=jlo, jhi=jhi, tok0=tok0, slot=slot: e.matmul(
                        out=psY1[:, jlo:jhi],
                        lhsT=wsb[:, slot * 1024 + ci * 128:slot * 1024 + ci * 128 + 128],
                        rhs=arnQ[:, ci * 2048 + tok0 + jlo:ci * 2048 + tok0 + jhi],
                        start=(ci == 0), stop=(ci == NC_E - 1)),
                        waits=mmw, inc="st")
                w1slot_user[slot] = lt
                mz = None
                if tau == 0:
                    mz = B.op("vector", lambda e, psY1=psY1: e.memset(
                        psY1[:, 0:1], 0.0), waits=[lt], inc="sv")
                elif tau == NTILE - 1:
                    mz = B.op("vector", lambda e, psY1=psY1: e.memset(
                        psY1[:, 257:258], 0.0), waits=[lt], inc="sv")
                cva = arnG[:, 256:514]
                cvb = arnG[:, 514:772]
                c1 = B.op("scalar", lambda e, hc=hc, cva=cva, psY1=psY1: e.activation(
                    out=cva[:, 0:256], in_=psY1[:, 0:256],
                    func=AF.Identity, scale=cw_sb[:, hc * 3:hc * 3 + 1],
                    bias=cb_sb[:, hc:hc + 1]), waits=[lt, mz], inc="ss")
                c2 = B.op("vector", lambda e, hc=hc, cva=cva, cvb=cvb, psY1=psY1: e.scalar_tensor_tensor(
                    out=cvb[:, 0:256], in0=psY1[:, 1:257],
                    scalar=cw_sb[:, hc * 3 + 1:hc * 3 + 2], in1=cva[:, 0:256],
                    op0=ALU.mult, op1=ALU.add), waits=[c1, lt, mz], inc="sv")
                c3 = B.op("vector", lambda e, hc=hc, cva=cva, cvb=cvb, psY1=psY1: e.scalar_tensor_tensor(
                    out=cva[:, 0:256], in0=psY1[:, 2:258],
                    scalar=cw_sb[:, hc * 3 + 2:hc * 3 + 3], in1=cvb[:, 0:256],
                    op0=ALU.mult, op1=ALU.add), waits=[c2], inc="sv")
                y1_evac[hc % 2] = c3
                gsl = B.op("scalar", lambda e, hc=hc, cva=cva: e.activation(
                    out=g_f[:, hc * 256:hc * 256 + 256], in_=cva[:, 0:256],
                    func=AF.Silu), waits=[c3], inc="ss")
                conv_done[hc] = gsl
                mpart = arnG[:, 0:256]
                gabs = arnG[:, 1284:1540]
                tab = B.op("scalar", lambda e, hc=hc, gabs=gabs: e.activation(
                    out=gabs, in_=g_f[:, hc * 256:hc * 256 + 256], func=AF.Abs),
                    waits=[gsl, mp_tok], inc="ss")
                if hc == 0:
                    mp_tok = B.op("vector", lambda e, mpart=mpart, gabs=gabs: e.tensor_copy(
                        out=mpart, in_=gabs), waits=[tab], inc="sv")
                else:
                    mp_tok = B.op("vector", lambda e, mpart=mpart, gabs=gabs: e.tensor_tensor(
                        out=mpart, in0=mpart, in1=gabs, op=ALU.max),
                        waits=[tab, mp_tok], inc="sv")
            if el == 0 and tau == 0 and "g" in dbg_t:
                dbg_gate.append(B.dma("scalar", dbg_t["g"][0:128, :],
                                      g_f[:, 0:8192], waits=[conv_done[NC_H - 1], mp_tok]))
            rowtoks = []
            for half in range(2):
                pv = psB[:, 1024 + half * 512:1024 + half * 512 + 128]
                w = [mp_tok, psb_rd[2 + half]] + y2_evac_prev
                t9 = B.op("tensor", lambda e, half=half, pv=pv: e.transpose(
                    out=pv, in_=arnG[:, half * 128:half * 128 + 128],
                    identity=idf[:]), waits=w, inc="st")
                rt = B.op("vector", lambda e, half=half, pv=pv: e.tensor_reduce(
                    out=st_[:, 70 + half:71 + half], in_=pv, axis=AX.X, op=ALU.max),
                    waits=[t9], inc="sv")
                rowtoks.append(rt)
                psb_rd[2 + half] = rt
            f1 = B.op("vector", lambda e: e.tensor_scalar(out=st_[:, 72:74],
                 in0=st_[:, 70:72], scalar1=1e-5, scalar2=None, op0=ALU.max),
                 waits=rowtoks, inc="sv")
            f2 = B.op("vector", lambda e: e.reciprocal(out=st_[:, 74:76],
                 in_=st_[:, 72:74]), waits=[f1], inc="sv")
            tsc2 = B.op("vector", lambda e: e.tensor_scalar(out=st_[:, 76:78],
                        in0=st_[:, 74:76], scalar1=127.0, scalar2=None,
                        op0=ALU.mult), waits=[f2], inc="sv")
            trs2 = B.op("vector", lambda e: e.tensor_scalar(out=st_[:, 78:80],
                        in0=st_[:, 72:74], scalar1=w2sr_col, scalar2=1.0 / 127.0,
                        op0=ALU.mult, op1=ALU.mult), waits=[f1], inc="sv")
            srow = arnS[0:1, 128:384]
            rrow = arnS[0:1, 384:640]
            bctoks = []
            for (cbase, tokn, row) in ((76, tsc2, srow), (78, trs2, rrow)):
                for half in range(2):
                    off11 = 1280 + (half % 2) * 512
                    t11 = B.op("tensor", lambda e, cbase=cbase, half=half, off11=off11: e.transpose(
                        out=psB[0:1, off11:off11 + 128],
                        in_=st_[:, cbase + half:cbase + half + 1], identity=idf[:]),
                        waits=[tokn, psb_rd[2 + (half % 2)]], inc="st")
                    bct = B.op("vector", lambda e, row=row, half=half, off11=off11: e.tensor_copy(
                        out=row[0:1, half * 128:half * 128 + 128],
                        in_=psB[0:1, off11:off11 + 128]),
                        waits=[t11], inc="sv")
                    bctoks.append(bct)
                    psb_rd[2 + (half % 2)] = bct
            s2bc = arnG[:, 772:1028]
            rs2bc = arnG[:, 1028:1284]
            t12 = B.op("tensor", lambda e, srow=srow: e.matmul(
                out=psB[:, 1024:1280], lhsT=onesf[0:1, 0:128], rhs=srow[0:1, 0:256],
                start=True, stop=True), waits=bctoks + [psb_rd[2]], inc="st")
            t13 = B.op("tensor", lambda e, rrow=rrow: e.matmul(
                out=psB[:, 1536:1792], lhsT=onesf[0:1, 0:128], rhs=rrow[0:1, 0:256],
                start=True, stop=True), waits=bctoks + [psb_rd[3]], inc="st")
            tbc_s = B.op("vector", lambda e, s2bc=s2bc: e.tensor_copy(
                out=s2bc, in_=psB[:, 1024:1280]), waits=[t12], inc="sv")
            tbc_r = B.op("vector", lambda e, rs2bc=rs2bc: e.tensor_copy(
                out=rs2bc, in_=psB[:, 1536:1792]), waits=[t13], inc="sv")
            psb_rd[2] = tbc_s
            psb_rd[3] = tbc_r

            y2_last = None
            tg2_prev = None
            tg3_prev = None
            for hc in range(NC_H):
                slot = hc % 4
                gqt = arnG[:, 1284:1540]
                gqi = arnG[:, 1540:1796].bitcast(I32)[:, 0:256]
                gqb = arnP[:, slot * 256:slot * 256 + 256]
                tg1 = B.op("vector", lambda e, hc=hc, gqt=gqt: e.tensor_tensor(
                    out=gqt, in0=g_f[:, hc * 256:hc * 256 + 256], in1=s2bc,
                    op=ALU.mult), waits=[tbc_s, conv_done[hc]], inc="sv")
                tg2 = B.op("vector", lambda e, gqt=gqt, gqi=gqi: e.tensor_copy(
                    out=gqi, in_=gqt), waits=[tg1, tg3_prev], inc="sv")
                tg2_prev = tg2
                tg3 = B.op("scalar", lambda e, gqi=gqi, gqb=gqb: e.copy(
                    out=gqb, in_=gqi),
                    waits=[tg2, gqb_user.get((hc - 2) % 4)], inc="ss")
                if el == 0 and tau == 0 and hc == 1 and "gq" in dbg_t:
                    gq_dump_tok = B.dma("scalar", dbg_t["gq"][0:128, :],
                                        arnP[:, 0:512], waits=[tg3])
                tg3_prev = tg3
                tw2 = B.dma("sync", wsb2[:, slot * 1024:slot * 1024 + 1024],
                            scrw2[hc * 128:hc * 128 + 128, :],
                            waits=[w2slot_user.get((hc - 2) % 4),
                                   (wo_done if tau == 0 and hc < 4 else None),
                                   w_sd_tok if el == 0 else None],
                            sem="sw2", chain=True)
                lt = None
                for oc in range(NC_E):
                    w = [tg3, tw2]
                    if hc == 0:
                        w += y2_evac_prev + [B.tok("sv"), y1_evac.get(0), y1_evac.get(1)]
                    lt = B.op("tensor", lambda e, oc=oc, gqb=gqb, slot=slot, hc=hc: e.matmul(
                        out=ybank(oc),
                        lhsT=wsb2[:, slot * 1024 + oc * 128:slot * 1024 + oc * 128 + 128],
                        rhs=gqb, start=(hc == 0), stop=(hc == NC_H - 1),
                        skip_group_check=True), waits=w, inc="st")
                gqb_user[slot] = lt
                w2slot_user[slot] = lt
                y2_last = lt
            if el == 0 and tau == 0 and "stats2" in dbg_t:
                dbg_gate.append(B.dma("scalar", dbg_t["stats2"][0:128, :],
                                      arnS[:, :], waits=[B.tok("sv")]))
            y2full = arnG[:, 2048:4096]
            y2_evac_prev = []
            for oc in range(NC_E):
                y2_evac_prev.append(B.op("vector", lambda e, oc=oc, y2full=y2full: e.tensor_tensor(
                    out=y2full[:, oc * 256:oc * 256 + 256], in0=ybank(oc), in1=rs2bc,
                    op=ALU.mult), waits=[y2_last, tbc_r], inc="sv"))
            if el == 0 and tau == 0 and "misc" in dbg_t:
                mb = arnH[:, 0:1024]
                mc1 = B.op("vector", lambda e, mb=mb: e.tensor_copy(
                    out=mb[:, 0:256], in_=s2bc), waits=[tbc_s], inc="sv")
                mc2 = B.op("vector", lambda e, mb=mb: e.tensor_copy(
                    out=mb[:, 256:512], in_=rs2bc), waits=[tbc_r, mc1], inc="sv")
                gq0 = arnG[:, 1796:1924].bitcast(BF16)
                mc3 = B.op("vector", lambda e, mb=mb, gq0=gq0: e.tensor_copy(
                    out=mb[:, 512:768], in_=gq0), waits=[gq_dbg_tok, mc2], inc="sv")
                mdt = B.dma("scalar", dbg_t["misc"][0:128, :], mb, waits=[mc3])
                dbg_gate.append(mdt)
                out_toks[0] = mdt
            if el == 0 and tau == 0 and "y2" in dbg_t:
                dbg_gate.append(B.dma("scalar", dbg_t["y2"][0:128, :],
                                      y2full, waits=y2_evac_prev))
            for half in range(2):
                y2tm = arnH[:, half * 1024:half * 1024 + 1024]
                xres = arnH[:, 2048:3072]
                rows0 = tau * 256 + half * 128
                cps = []
                for oc in range(NC_E):
                    slot = oc % 2
                    pv = psB[:, 1024 + slot * 512:1024 + slot * 512 + 128]
                    w = list(y2_evac_prev) + [psb_rd[2 + slot]]
                    if tau == 0 and half == 0 and oc == 0:
                        w.append(B.tok("ssp"))
                    t14 = B.op("tensor", lambda e, oc=oc, half=half, pv=pv, y2full=y2full: e.transpose(
                        out=pv, in_=y2full[:, oc * 256 + half * 128:oc * 256 + half * 128 + 128],
                        identity=idf[:]), waits=w, inc="st")
                    cp = B.op("vector", lambda e, oc=oc, pv=pv, y2tm=y2tm: e.tensor_copy(
                        out=y2tm[:, oc * 128:oc * 128 + 128], in_=pv),
                        waits=[t14, out_toks.get(half)], inc="sv")
                    cps.append(cp)
                    psb_rd[2 + slot] = cp
                txr = B.dma("sync", xres, scrx[el, rows0:rows0 + 128, :],
                            waits=[spills[rows0 // 128], B.tok("sv")],
                            sem="sxr", chain=True)
                ta3 = B.op("vector", lambda e, y2tm=y2tm, xres=xres: e.tensor_tensor(
                    out=y2tm, in0=y2tm, in1=xres, op=ALU.add), waits=[txr] + cps, inc="sv")
                x2b = arnG[:, 256:1280]
                ta4 = B.op("vector", lambda e, y2tm=y2tm, x2b=x2b: e.tensor_tensor(
                    out=x2b, in0=y2tm, in1=y2tm, op=ALU.mult),
                    waits=[ta3], inc="sv")
                g1 = B.op("vector", lambda e, y2tm=y2tm: e.tensor_reduce(
                    out=st_[:, 80:81], in_=y2tm, axis=AX.X, op=ALU.add),
                    waits=[ta3], inc="sv")
                g2 = B.op("vector", lambda e, x2b=x2b: e.tensor_reduce(
                    out=st_[:, 81:82], in_=x2b, axis=AX.X, op=ALU.add),
                    waits=[ta4], inc="sv")
                g3 = B.op("vector", lambda e: e.tensor_scalar(out=st_[:, 82:83],
                     in0=st_[:, 80:81], scalar1=1.0 / E, scalar2=None,
                     op0=ALU.mult), waits=[g1], inc="sv")
                g4 = B.op("vector", lambda e: e.tensor_scalar(out=st_[:, 83:84],
                     in0=st_[:, 81:82], scalar1=1.0 / E, scalar2=None,
                     op0=ALU.mult), waits=[g2], inc="sv")
                g5 = B.op("vector", lambda e: e.tensor_tensor(out=st_[:, 84:85],
                     in0=st_[:, 82:83], in1=st_[:, 82:83], op=ALU.mult),
                     waits=[g3], inc="sv")
                g6 = B.op("vector", lambda e: e.tensor_tensor(out=st_[:, 85:86],
                     in0=st_[:, 83:84], in1=st_[:, 84:85], op=ALU.subtract),
                     waits=[g4, g5], inc="sv")
                tve2 = B.op("vector", lambda e: e.tensor_scalar(out=st_[:, 86:87],
                            in0=st_[:, 85:86], scalar1=1e-5, scalar2=None,
                            op0=ALU.add), waits=[g6], inc="sv")
                rstd2 = rsqrt_chain(st_[:, 86:87], st_[:, 91:92],
                                    [st_[:, 87:88], st_[:, 88:89], st_[:, 89:90]],
                                    [tve2])
                tln2 = B.op("vector", lambda e, y2tm=y2tm: e.tensor_scalar(
                    out=y2tm, in0=y2tm, scalar1=st_[:, 82:83], scalar2=st_[:, 91:92],
                    op0=ALU.subtract, op1=ALU.mult), waits=[rstd2], inc="sv")
                out_toks[half] = B.dma("scalar", yout[el, rows0:rows0 + 128, :],
                                       y2tm, waits=[tln2], chain=True)

    B.op("scalar", None, waits=[B.tok("sq")])
    B.finalize()
    return nc


_IDB = np.eye(128, dtype=ml_dtypes.bfloat16)
_IDF = np.eye(128, dtype=np.float32)
_ONES = np.ones((128, 128), dtype=np.float32)


def core_inputs(L0_attn_x, L0_attn_Wk, L0_attn_Wq, L0_attn_Wo, L0_attn_bo,
                L0_ffn_w1, L0_ffn_conv_w, L0_ffn_conv_b, L0_ffn_w2):
    x = np.ascontiguousarray(np.asarray(L0_attn_x, dtype=np.float32))
    com = dict(
        wq=np.ascontiguousarray(np.asarray(L0_attn_Wq, dtype=np.float32)),
        wk=np.ascontiguousarray(np.asarray(L0_attn_Wk, dtype=np.float32)),
        wo=np.ascontiguousarray(np.asarray(L0_attn_Wo, dtype=np.float32)),
        bo=np.ascontiguousarray(np.asarray(L0_attn_bo, dtype=np.float32)),
        w1=np.ascontiguousarray(np.asarray(L0_ffn_w1, dtype=np.float32)),
        cw=np.ascontiguousarray(np.asarray(L0_ffn_conv_w, dtype=np.float32).reshape(DH, 3)),
        cb=np.ascontiguousarray(np.asarray(L0_ffn_conv_b, dtype=np.float32)),
        w2=np.ascontiguousarray(np.asarray(L0_ffn_w2, dtype=np.float32)),
        idb=_IDB, idf=_IDF, ones=_ONES)
    return [dict(x=np.ascontiguousarray(x[i * BEL:(i + 1) * BEL]), **com)
            for i in range(NCORES)]


def kernel(L0_attn_x, L0_attn_Wk, L0_attn_Wq, L0_attn_Wo, L0_attn_bo,
           L0_ffn_w1, L0_ffn_conv_w, L0_ffn_conv_b, L0_ffn_w2):
    in_maps = core_inputs(L0_attn_x, L0_attn_Wk, L0_attn_Wq, L0_attn_Wo,
                          L0_attn_bo, L0_ffn_w1, L0_ffn_conv_w, L0_ffn_conv_b,
                          L0_ffn_w2)
    nc = build_nc()
    res = run_bass_kernel_spmd(nc, in_maps, core_ids=list(range(NCORES)))
    return np.concatenate([np.asarray(r["y"], dtype=np.float32)
                           for r in res.results], axis=0)
